# version 08
# baseline (speedup 1.0000x reference)
import numpy as np
import ml_dtypes
from contextlib import ExitStack

import concourse.bass as bass
import concourse.mybir as mybir
from concourse.bass_utils import run_bass_kernel_spmd

F32 = mybir.dt.float32
BF16 = mybir.dt.bfloat16
F16 = mybir.dt.float16
I32 = mybir.dt.int32
U16 = mybir.dt.uint16
ALU = mybir.AluOpType
ACTF = mybir.ActivationFunctionType
AX = mybir.AxisListType

D = 2048
S = 2048
NT = 16
I0 = 5632
NI0 = 44
H1 = 8192
NH1 = 64
DENOM0 = float(np.sqrt(np.float32(2048.0))) / 2048.0
SC1 = 1.0 / float(np.sqrt(np.float32(128.0)))
LNEPS = 1e-5
RMS0EPS = 1e-6
RMS1EPS = 1e-8

ENGS = ("sync", "scalar", "vector", "gpsimd", "tensor")


class KB:
    _DSEM = {"sync": "d_sp", "scalar": "d_sc", "vector": "d_ve",
             "gpsimd": "d_gp", "tensor": "d_pe"}
    _QSEM = {"sync": "q_sp", "scalar": "q_sc", "gpsimd": "q_gp"}

    NQROT = 8

    def __init__(self, nc, es):
        self.nc = nc
        self.th = {e: [] for e in ENGS}
        self.semh = {}
        self.cnt = {}
        self.waited = {e: {} for e in ENGS}
        self.qrot = {}
        names = ["d_sp", "d_sc", "d_ve", "d_gp", "d_pe"]
        for q in ("q_sp", "q_sc", "q_gp"):
            self.qrot[q] = 0
            for r in range(self.NQROT):
                names.append(f"{q}{r}")
        for name in names:
            self.semh[name] = es.enter_context(nc.semaphore(name))
            self.cnt[name] = 0

    def op(self, eng, fn):
        self.th[eng].append(fn)

    def sig(self, eng, fn, dma=False):
        if dma:
            q = self._QSEM[eng]
            r = self.qrot[q] % self.NQROT
            self.qrot[q] += 1
            name = f"{q}{r}"
            amt = 16
            prev = self.cnt[name]
            if prev > 0:
                self.wait(eng, (name, prev))
        else:
            name = self._DSEM[eng]
            amt = 1
        self.cnt[name] += amt
        v = self.cnt[name]
        sem = self.semh[name]

        def fn2(e, fn=fn, sem=sem, amt=amt):
            fn(e).then_inc(sem, amt)
        self.th[eng].append(fn2)
        return (name, v)

    def wait(self, eng, *toks):
        best = {}
        for t in toks:
            if t is None:
                continue
            nm, v = t
            if v > best.get(nm, -1):
                best[nm] = v
        for nm, v in sorted(best.items()):
            if self.waited[eng].get(nm, 0) >= v:
                continue
            self.waited[eng][nm] = v
            sem = self.semh[nm]
            self.th[eng].append(lambda e, sem=sem, v=v: e.wait_ge(sem, v))

    def barrier(self):
        toks = []
        for e in ENGS:
            toks.append(self.sig(e, lambda x: x.drain(fusable=False)))
        for q in ("q_sp", "q_sc", "q_gp"):
            for r in range(self.NQROT):
                nm = f"{q}{r}"
                if self.cnt[nm] > 0:
                    toks.append((nm, self.cnt[nm]))
        for e in ENGS:
            self.wait(e, *toks)


def build_program(stop=None):
    nc = bass.Bass("TRN2", target_bir_lowering=False)
    es = ExitStack()
    es.__enter__()

    def din(name, shape, dt):
        return nc.dram_tensor(name, shape, dt, kind="ExternalInput").ap()

    def dout(name, shape, dt):
        return nc.dram_tensor(name, shape, dt, kind="ExternalOutput").ap()

    def dscr(name, shape, dt):
        return nc.dram_tensor(name, shape, dt).ap()

    x_t = din("x_t", [D, S], F32)
    wq0t = din("wq0t", [D, D], F32)
    wk0t = din("wk0t", [D, D], F32)
    wv0t = din("wv0t", [D, D], F32)
    wo0t = din("wo0t", [D, D], F32)
    gup_pk = din("gup_pk", [2 * I0, 256], I32)
    gup_sc = din("gup_sc", [2 * I0, 16], F16)
    dwn_pk = din("dwn_pk", [D, 704], I32)
    dwn_sc = din("dwn_sc", [D, 44], F16)
    wq1 = din("wq1", [D, D], F32)
    wk1 = din("wk1", [D, D], F32)
    wv1 = din("wv1", [D, D], F32)
    wo1 = din("wo1", [D, D], F32)
    bo1 = din("bo1", [128, 16], F32)
    wg1t = din("wg1t", [H1, D], F32)
    wu1t = din("wu1t", [H1, D], F32)
    wd1t = din("wd1t", [H1, D], F32)
    idn_i = din("idn", [128, 128], BF16)
    idnf_i = din("idnf", [128, 128], F32)
    maskw_i = din("maskw", [128, 1024], BF16)

    y_t = dout("y_t", [D, S], F32)

    wgu_sp = dscr("wgu_sp", [88 * 16, 16384], BF16)
    wdn_sp = dscr("wdn_sp", [16 * 44, 16384], BF16)
    wdn_nat = dscr("wdn_nat", [16, 128 * 5632], BF16)
    h_sp = dscr("h_sp", [NI0, 128 * S], BF16)
    p_sp = dscr("p_sp", [NH1, 128 * S], BF16)
    wds_sp = dscr("wds_sp", [16 * 64, 16384], BF16)
    x_sp = dscr("x_sp", [NT, 128 * S], BF16)
    row_sp = dscr("row_sp", [2, 2048], F32)

    def sb(name, shape, dt):
        return es.enter_context(nc.sbuf_tensor(name, shape, dt))

    xt = sb("xt", [128, 32768], BF16)
    bigA = sb("bigA", [128, 32768], BF16)
    bigB = sb("bigB", [128, 14464], BF16)
    wb = sb("wb", [128, 4096], BF16)
    fA = sb("fA", [128, 2048], BF16)
    fB = sb("fB", [128, 2048], BF16)
    rows = sb("rows", [64, 2048], F32)
    pkbuf = sb("pkbuf", [128, 1024], I32)
    u_i = sb("u_i", [128, 768], I32)
    sg_i = sb("sg_i", [128, 768], I32)
    sg_u = sb("sg_u", [128, 768], U16)
    f_b = sb("f_b", [128, 768], BF16)
    d1_b = sb("d1_b", [128, 768], BF16)
    d2_b = sb("d2_b", [128, 768], BF16)
    scb = sb("scb", [128, 88], F32)
    idn = sb("idn_s", [128, 128], BF16)
    idnf = sb("idnf_s", [128, 128], F32)
    maskw = sb("maskw_s", [128, 1024], BF16)
    bo = sb("bo_s", [128, 16], F32)
    onescol = sb("onescol", [128, 1], BF16)
    onesrow = sb("onesrow", [64, 128], F32)
    cmax = sb("cmax", [128, 16], F32)
    sqt = sb("sqt", [128, 16], F32)
    rsqt = sb("rsqt", [128, 16], F32)
    tmp16 = sb("tmp16", [16, 128], F32)
    recb = sb("recb", [128, 8], F32)
    qf = sb("qf", [128, 512], F32)
    epsb = sb("epsb", [128, 4], F32)

    pAh = es.enter_context(nc.psum_tensor("pA", [128, 2048], F32))
    pBh = es.enter_context(nc.psum_tensor("pB", [128, 2048], F32))
    pA = pAh[:, :]
    pB = pBh[:, :]

    kb = KB(nc, es)

    rsqb_bf = bass.AP(pkbuf.ap().tensor, 0,
                      [[1024, 128], [1, 1024]]).bitcast(BF16)
    qfb = bass.AP(u_i.ap().tensor, 0, [[768, 128], [1, 512]]).bitcast(F32)

    def bigb_row(c0, w):
        return bass.AP(bigB.ap().tensor, c0, [[14464, 1], [1, 2 * w]]).bitcast(F32)

    kb.sig("vector", lambda e: e.memset(onescol[:], 1.0))
    kb.sig("vector", lambda e: e.memset(onesrow[0:1, :], 1.0))
    kb.sig("vector", lambda e: e.memset(onesrow[32:33, :], 1.0))

    def set_vones(e):
        ins = None
        for t in range(16):
            ins = e.memset(bigB[:, 12288 + t * 129 + 128: 12288 + t * 129 + 129], 1.0)
        return ins
    kb.sig("vector", set_vones)
    kb.sig("vector", lambda e: e.memset(epsb[:, 0:1], LNEPS))
    kb.sig("vector", lambda e: e.memset(epsb[:, 1:2], RMS0EPS))
    kb.sig("vector", lambda e: e.memset(epsb[:, 2:3], RMS1EPS))
    tok_idn = kb.sig("sync", lambda e: e.dma_start(out=idn[:], in_=idn_i), dma=True)
    tok_idnf = kb.sig("sync", lambda e: e.dma_start(out=idnf[:], in_=idnf_i), dma=True)
    kb.sig("sync", lambda e: e.dma_start(out=maskw[:], in_=maskw_i), dma=True)
    kb.sig("sync", lambda e: e.dma_start(out=bo[:], in_=bo1), dma=True)
    for t in range(NT):
        kb.sig("gpsimd", lambda e, t=t: e.dma_start(
            out=xt[:, t * S:(t + 1) * S],
            in_=x_t[t * 128:(t + 1) * 128, :]), dma=True)
    kb.barrier()

    def cast_load(dst_ap, src_ap):
        return kb.sig("gpsimd", lambda e: e.dma_start(out=dst_ap, in_=src_ap),
                      dma=True)

    def wcol_load(wt, c0, slot):
        rl = wt.tensor.shape[1]
        nrt = wt.tensor.shape[0] // 128
        assert nrt == 16
        dst = bass.AP(wb.ap().tensor, slot * 2048,
                      [[4096, 128], [128, 16], [1, 128]])
        src = bass.AP(wt.tensor, c0, [[rl, 128], [rl * 128, 16], [1, 128]])
        return cast_load(dst, src)

    def wrow_load(wt, r0, slot):
        rl = wt.tensor.shape[1]
        dst = wb[:, slot * 2048: slot * 2048 + 2048]
        src = bass.AP(wt.tensor, r0 * rl, [[rl, 128], [1, 2048]])
        return cast_load(dst, src)

    def mm_group(out_ps, lhsT_of, rhs_of, ksteps, nw, pe_wait=()):
        kb.wait("tensor", *pe_wait)
        nch = [(n, min(n + 512, nw)) for n in range(0, nw, 512)]
        last = None
        for ki in range(ksteps):
            for (n0, n1) in nch:
                st = (ki == 0)
                sp = (ki == ksteps - 1)

                def f(e, ki=ki, n0=n0, n1=n1, st=st, sp=sp):
                    return e.matmul(out_ps[:, n0:n1], lhsT_of(ki),
                                    rhs_of(ki, n0, n1), start=st, stop=sp)
                if sp and n1 == nw:
                    last = kb.sig("tensor", f)
                else:
                    kb.op("tensor", f)
        return last

    def emit_stats(get_u, w, sum_ps, ssq_ps, utoks=None, nt=NT):
        tokpe = None
        if utoks is not None:
            kb.wait("vector", *utoks)
            kb.wait("tensor", *utoks)
        sq_rd = [None, None]
        for t in range(nt):
            sq = fA if t % 2 == 0 else fB
            kb.wait("vector", sq_rd[t % 2])
            tsq = kb.sig("vector", lambda e, t=t, sq=sq, w=w: e.tensor_tensor(
                sq[:, 0:w], get_u(t), get_u(t), ALU.mult))
            for (n0, n1) in [(n, min(n + 512, w)) for n in range(0, w, 512)]:
                def f1(e, t=t, n0=n0, n1=n1):
                    return e.matmul(sum_ps[:, n0:n1], onescol[:],
                                    get_u(t)[:, n0:n1], start=(t == 0),
                                    stop=(t == nt - 1))
                kb.op("tensor", f1)
            kb.wait("tensor", tsq)
            for (n0, n1) in [(n, min(n + 512, w)) for n in range(0, w, 512)]:
                def f2(e, t=t, n0=n0, n1=n1, sq=sq):
                    return e.matmul(ssq_ps[:, n0:n1], onescol[:],
                                    sq[:, n0:n1], start=(t == 0),
                                    stop=(t == nt - 1))
                tk = kb.sig("tensor", f2)
                if t == nt - 1 and n1 == w:
                    tokpe = tk
            sq_rd[t % 2] = tk
        return tokpe

    def emit_mr_rows_bc(tok_pe, sum_ps, ssq_ps, w, epsi, nfeat, m_ps, r_ps):
        scrA = bigb_row(10240, 512)
        scrB = bigb_row(11264, 512)
        kb.wait("vector", tok_pe)
        t1 = kb.sig("vector", lambda e: e.tensor_scalar(
            rows[0:1, 0:w], sum_ps[:, 0:w], 1.0 / nfeat, None, ALU.mult))
        tok_m = emit_bcast(rows[0:1, 0:w], m_ps, w, t1)
        tok_r = None
        t5 = None
        for c0 in range(0, w, 512):
            kb.wait("vector", t1, t5)
            t2 = kb.sig("vector", lambda e, c0=c0: e.tensor_tensor(
                scrA, rows[0:1, c0:c0 + 512], rows[0:1, c0:c0 + 512], ALU.mult))
            kb.wait("vector", t2, tok_r)
            t3 = kb.sig("vector", lambda e, c0=c0: e.scalar_tensor_tensor(
                scrB, ssq_ps[:, c0:c0 + 512], 1.0 / nfeat, scrA,
                ALU.mult, ALU.subtract))
            kb.wait("scalar", t3)
            t4 = kb.sig("scalar", lambda e: e.activation(
                scrA, scrB, ACTF.Ln, bias=epsb[0:1, epsi:epsi + 1]))
            kb.wait("scalar", t4)
            t5 = kb.sig("scalar", lambda e: e.activation(
                scrB, scrA, ACTF.Exp, scale=-0.5))
            rp = r_ps[:, c0:c0 + 512] if w > 512 else r_ps
            tok_r = emit_bcast(scrB, rp, 512, t5, tok_r)
        return tok_m, tok_r

    def emit_bcast(row_ap, out_ps, w, *pe_wait):
        kb.wait("tensor", *pe_wait)
        bp = row_ap.base_partition()
        ones = onesrow[bp:bp + 1, :]
        last = None
        for (n0, n1) in [(n, min(n + 512, w)) for n in range(0, w, 512)]:
            def f(e, n0=n0, n1=n1, ones=ones):
                return e.matmul(out_ps[:, n0:n1], ones,
                                row_ap[:, n0:n1], start=True, stop=True)
            if n1 == w:
                last = kb.sig("tensor", f)
            else:
                kb.op("tensor", f)
        return last

    def phase1(pump):
        tok_pA = None
        tok_pB = None
        tok_s0 = None
        tok_s1 = None
        t_vh = None
        t_scan_prev = None
        for et in range(NT):
            pump(6)
            c0 = et * 128
            kb.wait("gpsimd", tok_s0)
            t_wk = wcol_load(wk0t, c0, 0)
            kb.wait("gpsimd", tok_s1)
            t_wv = wcol_load(wv0t, c0, 1)

            def lh(ki, sl):
                return wb[:, sl * 2048 + ki * 128: sl * 2048 + ki * 128 + 128]

            def rh(ki, n0, n1):
                return xt[:, ki * S + n0: ki * S + n1]

            t_mmK = mm_group(pA, lambda ki: lh(ki, 0), rh, NT, S,
                             pe_wait=(t_wk, tok_pA))
            kb.wait("scalar", t_mmK, t_vh)
            t_k = kb.sig("scalar", lambda e: e.copy(fA[:], pA))
            tok_s0 = t_mmK
            kb.wait("gpsimd", t_mmK)
            t_wq = wcol_load(wq0t, c0, 0)
            t_mmV = mm_group(pB, lambda ki: lh(ki, 1), rh, NT, S,
                             pe_wait=(t_wv, tok_pB))
            tok_s1 = t_mmV
            kb.wait("vector", t_mmV, t_k, t_scan_prev)
            t_kv = kb.sig("vector", lambda e: e.tensor_tensor(
                fB[:], pB, fA[:], ALU.mult))
            tok_pB = t_kv
            kb.wait("vector", t_kv)
            t_scan = kb.sig("vector", lambda e: e.tensor_tensor_scan(
                fA[:], fB[:], fB[:], 0.0, ALU.add, ALU.bypass))
            t_scan_prev = t_scan
            t_mmQ = mm_group(pA, lambda ki: lh(ki, 0), rh, NT, S,
                             pe_wait=(t_wq, t_k))
            tok_s0 = t_mmQ
            kb.wait("vector", t_scan, t_mmQ)
            t_vh = kb.sig("vector", lambda e, et=et: e.scalar_tensor_tensor(
                bigA[:, et * S:(et + 1) * S], fA[:], DENOM0, pA,
                ALU.mult, ALU.mult))
            tok_pA = t_vh

        utoks = []
        ps_tok = [tok_pA, tok_pB]
        slot_tok = [tok_s0, tok_s1]
        for ft in range(NT):
            pump(6)
            sl = ft % 2
            kb.wait("gpsimd", slot_tok[sl])
            t_wo = wcol_load(wo0t, ft * 128, sl)
            ps = pA if sl == 0 else pB
            t_mm = mm_group(ps,
                            lambda ki, sl=sl: wb[:, sl * 2048 + ki * 128: sl * 2048 + ki * 128 + 128],
                            lambda ki, n0, n1: bigA[:, ki * S + n0: ki * S + n1],
                            NT, S, pe_wait=(t_wo, t_vh, ps_tok[sl]))
            slot_tok[sl] = t_mm
            tb = fA if sl == 0 else fB
            kb.wait("vector", t_mm)
            t_u1 = kb.sig("vector", lambda e, ps=ps, ft=ft, tb=tb: e.tensor_tensor(
                tb[:], ps, xt[:, ft * S:(ft + 1) * S], ALU.add))
            ps_tok[sl] = t_u1
            kb.wait("vector", t_u1)
            t_u2 = kb.sig("vector", lambda e, ft=ft, tb=tb: e.tensor_copy(
                xt[:, ft * S:(ft + 1) * S], tb[:]))
            utoks.append(t_u2)
        return utoks

    def emit_full_ln(utoks, epsi):
        tok_pe = emit_stats(lambda t: xt[:, t * S:(t + 1) * S], 2048,
                            pA[0:1, :], pB[0:1, :], utoks=utoks)
        tok_m, tok_rb = emit_mr_rows_bc(tok_pe, pA[0:1, :], pB[0:1, :],
                                        2048, epsi, 2048, pA, pB)
        ntoks = []
        for t in range(NT):
            tb = fA if t % 2 == 0 else fB
            kb.wait("vector", tok_m, tok_rb)
            t_s = kb.sig("vector", lambda e, t=t, tb=tb: e.tensor_tensor(
                tb[:], xt[:, t * S:(t + 1) * S], pA, ALU.subtract))
            kb.wait("vector", t_s)
            t_n = kb.sig("vector", lambda e, t=t, tb=tb: e.tensor_tensor(
                xt[:, t * S:(t + 1) * S], tb[:], pB, ALU.mult))
            ntoks.append(t_n)
        return ntoks

    dec_lane_or = [None, None, None, None]
    dec_lane_st2 = [None, None, None, None]
    dec_lane_st3 = [None, None, None, None]
    dec_lane_st6 = [None, None, None, None]
    dec_asm_free = {0: None, 2048: None, 4096: None}

    def emit_decode(t_pk, pko, C, FD, G, ng, sco, asm0, scaled_t, scaled0,
                    tok_scaled_free, t_sc):
        units = []
        for j in range(8):
            for hf in range(C // FD):
                units.append((j, hf * FD))
        n = len(units)
        st_tok = [[None] * 8 for _ in range(G)]
        last_or = None
        for w0 in range(0, n, G):
            wave = units[w0:w0 + G]
            for l, (j, co) in enumerate(wave):
                lo = l * FD
                kb.wait("vector", dec_lane_or[l], dec_lane_st3[l], t_pk,
                        dec_asm_free[asm0])
                st_tok[l][0] = kb.sig("vector", lambda e, j=j, co=co, lo=lo, pko=pko, FD=FD: e.tensor_scalar(
                    u_i[:, lo:lo + FD], pkbuf[:, pko + co:pko + co + FD],
                    4 * j, 7, ALU.logical_shift_right, ALU.bitwise_and))
            for l, (j, co) in enumerate(wave):
                lo = l * FD
                sbit = 4 * j + 3
                msk = (1 << sbit) if sbit < 31 else -(1 << 31)
                kb.wait("vector", dec_lane_st2[l])
                if sbit <= 15:
                    st_tok[l][1] = kb.sig("vector", lambda e, co=co, lo=lo, pko=pko, FD=FD, msk=msk, sh=15 - sbit: e.tensor_scalar(
                        sg_i[:, lo:lo + FD], pkbuf[:, pko + co:pko + co + FD],
                        msk, sh, ALU.bitwise_and, ALU.logical_shift_left))
                else:
                    st_tok[l][1] = kb.sig("vector", lambda e, co=co, lo=lo, pko=pko, FD=FD, msk=msk, sh=sbit - 15: e.tensor_scalar(
                        sg_i[:, lo:lo + FD], pkbuf[:, pko + co:pko + co + FD],
                        msk, sh, ALU.bitwise_and, ALU.logical_shift_right))
            for l in range(len(wave)):
                lo = l * FD
                kb.wait("gpsimd", st_tok[l][1], dec_lane_or[l])
                st_tok[l][2] = kb.sig("gpsimd", lambda e, lo=lo, FD=FD: e.tensor_copy(
                    sg_u[:, lo:lo + FD], sg_i[:, lo:lo + FD]))
                dec_lane_st2[l] = st_tok[l][2]
            for l in range(len(wave)):
                lo = l * FD
                kb.wait("gpsimd", st_tok[l][0], dec_lane_st6[l])
                st_tok[l][3] = kb.sig("gpsimd", lambda e, lo=lo, FD=FD: e.tensor_copy(
                    f_b[:, lo:lo + FD], u_i[:, lo:lo + FD]))
                dec_lane_st3[l] = st_tok[l][3]
            for l in range(len(wave)):
                lo = l * FD
                kb.wait("vector", st_tok[l][3])
                st_tok[l][4] = kb.sig("vector", lambda e, lo=lo, FD=FD: e.tensor_scalar(
                    d1_b[:, lo:lo + FD], f_b[:, lo:lo + FD], 2.0, 8.0,
                    ALU.mult, ALU.subtract))
            for l in range(len(wave)):
                lo = l * FD
                kb.wait("vector", st_tok[l][4])
                st_tok[l][5] = kb.sig("vector", lambda e, lo=lo, FD=FD: e.scalar_tensor_tensor(
                    d2_b[:, lo:lo + FD], f_b[:, lo:lo + FD], 2.0,
                    d1_b[:, lo:lo + FD], ALU.subtract, ALU.max))
            for l in range(len(wave)):
                lo = l * FD
                kb.wait("vector", st_tok[l][5])
                st_tok[l][6] = kb.sig("vector", lambda e, lo=lo, FD=FD: e.scalar_tensor_tensor(
                    d1_b[:, lo:lo + FD], f_b[:, lo:lo + FD], 0.5,
                    d2_b[:, lo:lo + FD], ALU.mult, ALU.max))
                dec_lane_st6[l] = st_tok[l][6]
            for l, (j, co) in enumerate(wave):
                lo = l * FD
                kb.wait("vector", st_tok[l][6], st_tok[l][2])

                def orw(e, asm0=asm0, co=co, j=j, FD=FD, lo=lo):
                    outap = bass.AP(bigB.ap().tensor, asm0 + 8 * co + j,
                                    [[14464, 128], [8, FD]]).bitcast(U16)
                    return e.tensor_tensor(
                        outap, d1_b[:, lo:lo + FD].bitcast(U16),
                        sg_u[:, lo:lo + FD], ALU.bitwise_or)
                last_or = kb.sig("vector", orw)
                dec_lane_or[l] = last_or
        kb.wait("vector", last_or, t_sc, tok_scaled_free)
        sc_last = None
        for g in range(ng):
            def fsc(e, asm0=asm0, g=g, sco=sco, scaled0=scaled0, scaled_t=scaled_t):
                return e.tensor_scalar(
                    scaled_t[:, scaled0 + g * 128: scaled0 + g * 128 + 128],
                    bigB[:, asm0 + g * 128: asm0 + g * 128 + 128],
                    scb[:, sco + g: sco + g + 1], None, ALU.mult)
            sc_last = kb.sig("vector", fsc)
        dec_asm_free[asm0] = sc_last
        return sc_last, last_or

    def dwn_gen():
        tok_pk = None
        tok_sc = None
        tok_nat = None
        G, FD, C, ng = 2, 352, 704, 44
        for ot in range(16):
            kb.wait("sync", tok_pk)
            t_pk = kb.sig("sync", lambda e, ot=ot: e.dma_start(
                out=pkbuf[:, 0:704],
                in_=dwn_pk[ot * 128:(ot + 1) * 128, :]), dma=True)
            kb.wait("gpsimd", tok_sc)
            t_sc = kb.sig("gpsimd", lambda e, ot=ot: e.dma_start(
                out=scb[:, 0:44],
                in_=dwn_sc[ot * 128:(ot + 1) * 128, :]), dma=True)
            yield
            units = [(j, hf * FD) for j in range(8) for hf in range(C // FD)]
            st_tok = [[None] * 8 for _ in range(G)]
            last_or = None
            for w0 in range(0, len(units), G):
                wave = units[w0:w0 + G]
                for l, (j, co) in enumerate(wave):
                    lo = l * FD
                    kb.wait("vector", dec_lane_or[l], dec_lane_st3[l], t_pk,
                            dec_asm_free[0])
                    st_tok[l][0] = kb.sig("vector", lambda e, j=j, co=co, lo=lo: e.tensor_scalar(
                        u_i[:, lo:lo + FD], pkbuf[:, co:co + FD],
                        4 * j, 7, ALU.logical_shift_right, ALU.bitwise_and))
                for l, (j, co) in enumerate(wave):
                    lo = l * FD
                    sbit = 4 * j + 3
                    msk = (1 << sbit) if sbit < 31 else -(1 << 31)
                    kb.wait("vector", dec_lane_st2[l])
                    if sbit <= 15:
                        st_tok[l][1] = kb.sig("vector", lambda e, co=co, lo=lo, msk=msk, sh=15 - sbit: e.tensor_scalar(
                            sg_i[:, lo:lo + FD], pkbuf[:, co:co + FD],
                            msk, sh, ALU.bitwise_and, ALU.logical_shift_left))
                    else:
                        st_tok[l][1] = kb.sig("vector", lambda e, co=co, lo=lo, msk=msk, sh=sbit - 15: e.tensor_scalar(
                            sg_i[:, lo:lo + FD], pkbuf[:, co:co + FD],
                            msk, sh, ALU.bitwise_and, ALU.logical_shift_right))
                for l in range(len(wave)):
                    lo = l * FD
                    kb.wait("gpsimd", st_tok[l][1], dec_lane_or[l])
                    st_tok[l][2] = kb.sig("gpsimd", lambda e, lo=lo: e.tensor_copy(
                        sg_u[:, lo:lo + FD], sg_i[:, lo:lo + FD]))
                    dec_lane_st2[l] = st_tok[l][2]
                for l in range(len(wave)):
                    lo = l * FD
                    kb.wait("gpsimd", st_tok[l][0], dec_lane_st6[l])
                    st_tok[l][3] = kb.sig("gpsimd", lambda e, lo=lo: e.tensor_copy(
                        f_b[:, lo:lo + FD], u_i[:, lo:lo + FD]))
                    dec_lane_st3[l] = st_tok[l][3]
                for l in range(len(wave)):
                    lo = l * FD
                    kb.wait("vector", st_tok[l][3])
                    st_tok[l][4] = kb.sig("vector", lambda e, lo=lo: e.tensor_scalar(
                        d1_b[:, lo:lo + FD], f_b[:, lo:lo + FD], 2.0, 8.0,
                        ALU.mult, ALU.subtract))
                for l in range(len(wave)):
                    lo = l * FD
                    kb.wait("vector", st_tok[l][4])
                    st_tok[l][5] = kb.sig("vector", lambda e, lo=lo: e.scalar_tensor_tensor(
                        d2_b[:, lo:lo + FD], f_b[:, lo:lo + FD], 2.0,
                        d1_b[:, lo:lo + FD], ALU.subtract, ALU.max))
                for l in range(len(wave)):
                    lo = l * FD
                    kb.wait("vector", st_tok[l][5])
                    st_tok[l][6] = kb.sig("vector", lambda e, lo=lo: e.scalar_tensor_tensor(
                        d1_b[:, lo:lo + FD], f_b[:, lo:lo + FD], 0.5,
                        d2_b[:, lo:lo + FD], ALU.mult, ALU.max))
                    dec_lane_st6[l] = st_tok[l][6]
                for l, (j, co) in enumerate(wave):
                    lo = l * FD
                    kb.wait("vector", st_tok[l][6], st_tok[l][2])

                    def orw(e, co=co, j=j, lo=lo):
                        outap = bass.AP(bigB.ap().tensor, 8 * co + j,
                                        [[14464, 128], [8, FD]]).bitcast(U16)
                        return e.tensor_tensor(
                            outap, d1_b[:, lo:lo + FD].bitcast(U16),
                            sg_u[:, lo:lo + FD], ALU.bitwise_or)
                    last_or = kb.sig("vector", orw)
                    dec_lane_or[l] = last_or
                yield
            tok_pk = last_or
            kb.wait("vector", last_or, t_sc, tok_nat)
            sc_last = None
            for g in range(ng):
                def fsc(e, g=g):
                    return e.tensor_scalar(
                        bigB[:, 5632 + g * 128: 5632 + g * 128 + 128],
                        bigB[:, g * 128: g * 128 + 128],
                        scb[:, g: g + 1], None, ALU.mult)
                sc_last = kb.sig("vector", fsc)
            dec_asm_free[0] = sc_last
            tok_sc = sc_last
            yield
            kb.wait("sync", sc_last)
            tok_nat = kb.sig("sync", lambda e, ot=ot: e.dma_start(
                out=bass.AP(wdn_nat.tensor, ot * 128 * 5632,
                            [[5632, 128], [1, 5632]]),
                in_=bigB[:, 5632:11264]), dma=True)
            yield

    def phase2a(tok_idn):
        tok_ld = [None, None]
        tok_stage = [None, None]
        tok_psT = None
        tok_rd = [None, None]
        t_sp = None
        for ot in range(16):
            slot = ot % 2
            kb.wait("sync", tok_rd[slot])
            t_ld = kb.sig("sync", lambda e, ot=ot, slot=slot: e.dma_start(
                out=bigA[:, slot * 5632: slot * 5632 + 5632],
                in_=bass.AP(wdn_nat.tensor, ot * 128 * 5632,
                            [[5632, 128], [1, 5632]])), dma=True)
            t_tp = None
            for g0 in range(0, 44, 16):
                gc = min(16, 44 - g0)
                sl2 = (g0 // 16) % 2
                kb.wait("tensor", t_ld, tok_psT, tok_idn)
                for gg in range(gc):
                    def ftp(e, g0=g0, gg=gg, slot=slot):
                        dst = pB[:, gg * 64: gg * 64 + 64].bitcast(BF16)
                        return e.transpose(
                            dst,
                            bigA[:, slot * 5632 + (g0 + gg) * 128: slot * 5632 + (g0 + gg) * 128 + 128],
                            idn[:])
                    if gg == gc - 1:
                        t_tp = kb.sig("tensor", ftp)
                    else:
                        kb.op("tensor", ftp)
                kb.wait("scalar", t_tp, tok_stage[sl2])
                t_ev = kb.sig("scalar", lambda e, gc=gc, sl2=sl2: e.copy(
                    bigA[:, 16384 + sl2 * 2048: 16384 + sl2 * 2048 + gc * 128],
                    pB[:, 0:gc * 64].bitcast(BF16)))
                tok_psT = t_ev
                kb.wait("sync", t_ev)

                def fsp(e, ot=ot, g0=g0, gc=gc, sl2=sl2):
                    dst = bass.AP(wdn_sp.tensor, (ot * 44 + g0) * 16384,
                                  [[128, 128], [16384, gc], [1, 128]])
                    srcp = bass.AP(bigA.ap().tensor, 16384 + sl2 * 2048,
                                   [[32768, 128], [128, gc], [1, 128]])
                    return e.dma_start(out=dst, in_=srcp)
                t_sp = kb.sig("sync", fsp, dma=True)
                tok_stage[sl2] = t_sp
            tok_rd[slot] = t_tp
        return t_sp

    def phase2b():
        LH = [16384, 20480]
        tok_pkslot = [None, None]
        tok_scslot2 = [None, None]
        tok_scaled_g = None
        tok_scaled_u = None
        tok_lh = [None, None]
        tok_evA = None
        tok_pA = None
        tok_pB = None
        tok_g = None
        tok_h = [None, None]
        ev_toks = [None, None]

        def load_pair(it):
            slot = it % 2
            kb.wait("sync", tok_pkslot[slot])
            t1 = kb.sig("sync", lambda e, it=it, slot=slot: e.dma_start(
                out=pkbuf[:, slot * 512: slot * 512 + 256],
                in_=gup_pk[it * 128:(it + 1) * 128, :]), dma=True)
            t2 = kb.sig("sync", lambda e, it=it, slot=slot: e.dma_start(
                out=pkbuf[:, slot * 512 + 256: slot * 512 + 512],
                in_=gup_pk[(it + 44) * 128:(it + 45) * 128, :]), dma=True)
            kb.wait("gpsimd", tok_scslot2[slot])
            t3 = kb.sig("gpsimd", lambda e, it=it, slot=slot: e.dma_start(
                out=scb[:, slot * 16: slot * 16 + 16],
                in_=gup_sc[it * 128:(it + 1) * 128, :]), dma=True)
            t4 = kb.sig("gpsimd", lambda e, it=it, slot=slot: e.dma_start(
                out=scb[:, 32 + slot * 16: 32 + slot * 16 + 16],
                in_=gup_sc[(it + 44) * 128:(it + 45) * 128, :]), dma=True)
            return t1, t2, t3, t4

        def dec_pair(it, toks):
            slot = it % 2
            nonlocal tok_scaled_g, tok_scaled_u
            t1, t2, t3, t4 = toks
            sg_tok, or_g = emit_decode(t1, slot * 512, 256, 256, 3, 16,
                                       slot * 16, 0, bigB, 4096,
                                       tok_scaled_g, t3)
            su_tok, or_u = emit_decode(t2, slot * 512 + 256, 256, 256, 3, 16,
                                       32 + slot * 16, 2048, bigB, 6144,
                                       tok_scaled_u, t4)
            tok_scslot2[slot] = su_tok
            return sg_tok, su_tok, or_u

        def tp_pair(it, sg_tok, su_tok):
            nonlocal tok_evA, tok_scaled_g, tok_scaled_u
            ls = it % 2
            ts_g = None
            ts_u = None
            kb.wait("tensor", sg_tok, tok_g, tok_lh[ls])
            for gg in range(16):
                def ftp(e, gg=gg):
                    return e.transpose(pA[:, gg * 64: gg * 64 + 64].bitcast(BF16),
                                       bigB[:, 4096 + gg * 128: 4096 + gg * 128 + 128],
                                       idn[:])
                if gg == 15:
                    ts_g = kb.sig("tensor", ftp)
                else:
                    kb.op("tensor", ftp)
            tok_scaled_g = ts_g
            kb.wait("tensor", su_tok)
            for gg in range(16):
                def ftp(e, gg=gg):
                    return e.transpose(pA[:, 1024 + gg * 64: 1024 + gg * 64 + 64].bitcast(BF16),
                                       bigB[:, 6144 + gg * 128: 6144 + gg * 128 + 128],
                                       idn[:])
                if gg == 15:
                    ts_u = kb.sig("tensor", ftp)
                else:
                    kb.op("tensor", ftp)
            tok_scaled_u = ts_u
            kb.wait("scalar", ts_g)
            e1 = kb.sig("scalar", lambda e, ls=ls: e.copy(
                bigA[:, LH[ls]:LH[ls] + 2048], pA[:, 0:1024].bitcast(BF16)))
            kb.wait("scalar", ts_u)
            e2 = kb.sig("scalar", lambda e, ls=ls: e.copy(
                bigA[:, LH[ls] + 2048:LH[ls] + 4096],
                pA[:, 1024:2048].bitcast(BF16)))
            tok_evA = e2
            return e1, e2

        toks = load_pair(0)
        sg_tok, su_tok, or_u = dec_pair(0, toks)
        tok_pkslot[0] = or_u
        ev_toks = tp_pair(0, sg_tok, su_tok)

        for it in range(NI0):
            ls = it % 2
            my_ev = ev_toks
            if it + 1 < NI0:
                toks = load_pair(it + 1)
            t_gm = mm_group(pA, lambda ki, ls=ls: bigA[:, LH[ls] + ki * 128: LH[ls] + ki * 128 + 128],
                            lambda ki, n0, n1: xt[:, ki * S + n0: ki * S + n1],
                            NT, S, pe_wait=(my_ev[0], my_ev[1], tok_pA))
            t_um = mm_group(pB, lambda ki, ls=ls: bigA[:, LH[ls] + 2048 + ki * 128: LH[ls] + 2048 + ki * 128 + 128],
                            lambda ki, n0, n1: xt[:, ki * S + n0: ki * S + n1],
                            NT, S, pe_wait=(tok_pB,))
            tok_lh[ls] = t_um
            kb.wait("scalar", t_gm, tok_g)
            t_si = kb.sig("scalar", lambda e: e.activation(
                bigB[:, 8192:10240], pA, ACTF.Silu))
            tok_pA = t_si
            tok_g = t_si
            if it + 1 < NI0:
                sg_tok, su_tok, or_u = dec_pair(it + 1, toks)
                tok_pkslot[(it + 1) % 2] = or_u
                ev_toks = tp_pair(it + 1, sg_tok, su_tok)
            hf = fA if it % 2 == 0 else fB
            kb.wait("vector", t_si, t_um, tok_h[it % 2])
            t_hm = kb.sig("vector", lambda e, hf=hf: e.tensor_tensor(
                hf[:], pB, bigB[:, 8192:10240], ALU.mult))
            tok_pB = t_hm
            tok_g = t_hm
            kb.wait("gpsimd", t_hm)
            t_hs = kb.sig("gpsimd", lambda e, it=it, hf=hf: e.dma_start(
                out=bass.AP(h_sp.tensor, it * 128 * S, [[S, 128], [1, S]]),
                in_=hf[:]), dma=True)
            tok_h[it % 2] = t_hs

        kb.barrier()

        for ch in range(4):
            c0 = ch * 512
            t_hc = kb.sig("sync", lambda e, c0=c0: e.dma_start(
                out=bass.AP(bigA.ap().tensor, 0,
                            [[32768, 128], [512, 44], [1, 512]]),
                in_=bass.AP(h_sp.tensor, c0,
                            [[S, 128], [128 * S, 44], [1, 512]])), dma=True)
            tok_wsl = [None, None]
            tok_ybank = [None, None, None, None]
            ss_tok = None
            for dm in range(16):
                bank = dm % 4
                mmlast = None
                for gs in range(4):
                    sl = (dm * 4 + gs) % 2
                    kb.wait("sync", tok_wsl[sl])
                    t_l = kb.sig("sync", lambda e, dm=dm, gs=gs, sl=sl: e.dma_start(
                        out=bass.AP(wb.ap().tensor, sl * 2048,
                                    [[4096, 128], [128, 11], [1, 128]]),
                        in_=bass.AP(wdn_sp.tensor, (dm * 44 + gs * 11) * 16384,
                                    [[128, 128], [16384, 11], [1, 128]])), dma=True)
                    kb.wait("tensor", t_l, t_hc,
                            tok_ybank[bank] if gs == 0 else None)
                    for ki in range(11):
                        st = (gs == 0 and ki == 0)
                        sp = (gs == 3 and ki == 10)

                        def f(e, ki=ki, gs=gs, sl=sl, bank=bank, st=st, sp=sp):
                            return e.matmul(
                                pA[:, bank * 512: bank * 512 + 512],
                                wb[:, sl * 2048 + ki * 128: sl * 2048 + ki * 128 + 128],
                                bass.AP(bigA.ap().tensor, (gs * 11 + ki) * 512,
                                        [[32768, 128], [1, 512]]),
                                start=st, stop=sp)
                        if ki == 10:
                            mmlast = kb.sig("tensor", f)
                        else:
                            kb.op("tensor", f)
                    tok_wsl[sl] = mmlast
                kb.wait("scalar", mmlast)
                t_ybf = kb.sig("scalar", lambda e, dm=dm, bank=bank: e.copy(
                    bigB[:, 2048 + dm * 512: 2048 + dm * 512 + 512],
                    pA[:, bank * 512: bank * 512 + 512]))
                kb.wait("vector", t_ybf)
                sq = fA if dm % 2 == 0 else fB
                t_sq = kb.sig("vector", lambda e, dm=dm, sq=sq: e.tensor_tensor(
                    sq[:, 0:512],
                    bigB[:, 2048 + dm * 512: 2048 + dm * 512 + 512],
                    bigB[:, 2048 + dm * 512: 2048 + dm * 512 + 512], ALU.mult))
                kb.wait("tensor", t_sq)

                def fss(e, dm=dm, sq=sq):
                    return e.matmul(pB[0:1, 0:512], onescol[:], sq[:, 0:512],
                                    start=(dm == 0), stop=(dm == 15))
                if dm == 15:
                    ss_tok = kb.sig("tensor", fss)
                else:
                    kb.op("tensor", fss)
                tok_ybank[bank] = t_ybf
            kb.wait("vector", ss_tok)
            t_r1 = kb.sig("vector", lambda e: e.tensor_scalar(
                rows[0:1, 0:512], pB[0:1, 0:512], 1.0 / 2048.0, None, ALU.mult))
            kb.wait("scalar", t_r1)
            t_r2 = kb.sig("scalar", lambda e: e.activation(
                bigb_row(10240, 512), rows[0:1, 0:512], ACTF.Ln,
                bias=epsb[0:1, 1:2]))
            kb.wait("scalar", t_r2)
            t_r3 = kb.sig("scalar", lambda e: e.activation(
                bigb_row(11264, 512), bigb_row(10240, 512), ACTF.Exp, scale=-0.5))
            kb.wait("tensor", t_r3, t_r1)
            t_rb = kb.sig("tensor", lambda e: e.matmul(
                pB[:, 512:1024], onesrow[0:1, :], bigb_row(11264, 512),
                start=True, stop=True))
            utoks = []
            for dm in range(16):
                tb = fA if dm % 2 == 0 else fB
                kb.wait("vector", t_rb)
                t1 = kb.sig("vector", lambda e, dm=dm, tb=tb: e.tensor_tensor(
                    tb[:, 0:512],
                    bigB[:, 2048 + dm * 512: 2048 + dm * 512 + 512],
                    pB[:, 512:1024], ALU.mult))
                kb.wait("vector", t1)
                t2 = kb.sig("vector", lambda e, dm=dm, tb=tb, c0=c0: e.tensor_tensor(
                    bigB[:, 2048 + dm * 512: 2048 + dm * 512 + 512],
                    tb[:, 0:512], xt[:, dm * S + c0: dm * S + c0 + 512],
                    ALU.add))
                utoks.append(t2)
            tok_pe = emit_stats(
                lambda t: bigB[:, 2048 + t * 512: 2048 + t * 512 + 512], 512,
                pA[0:1, 0:512], pA[0:1, 512:1024], utoks=utoks)
            tok_m, tok_rb2 = emit_mr_rows_bc(
                tok_pe, pA[0:1, 0:512], pA[0:1, 512:1024], 512, 0, 2048,
                pA[:, 1024:1536], pA[:, 1536:2048])
            for t in range(NT):
                tb = fA if t % 2 == 0 else fB
                kb.wait("vector", tok_m, tok_rb2)
                t_s = kb.sig("vector", lambda e, t=t, tb=tb: e.tensor_tensor(
                    tb[:, 0:512],
                    bigB[:, 2048 + t * 512: 2048 + t * 512 + 512],
                    pA[:, 1024:1536], ALU.subtract))
                kb.wait("vector", t_s)
                kb.sig("vector", lambda e, t=t, tb=tb, c0=c0: e.tensor_tensor(
                    xt[:, t * S + c0: t * S + c0 + 512], tb[:, 0:512],
                    pA[:, 1536:2048], ALU.mult))
            kb.barrier()

    def phase3():
        for h in range(16):
            t_wq = wcol_load(wq1, h * 128, 0)
            t_q = mm_group(pA, lambda ki: wb[:, ki * 128: ki * 128 + 128],
                           lambda ki, n0, n1: xt[:, ki * S + n0: ki * S + n1],
                           NT, S, pe_wait=(t_wq,))
            kb.wait("scalar", t_q)
            t_qe = kb.sig("scalar", lambda e: e.copy(bigB[:, 0:2048], pA))
            kb.wait("gpsimd", t_q)
            t_wk = wcol_load(wk1, h * 128, 1)
            t_kmm = mm_group(pB, lambda ki: wb[:, 2048 + ki * 128: 2048 + ki * 128 + 128],
                             lambda ki, n0, n1: xt[:, ki * S + n0: ki * S + n1],
                             NT, S, pe_wait=(t_wk,))
            kb.wait("scalar", t_kmm)
            t_ke = kb.sig("scalar", lambda e: e.copy(bigB[:, 2048:4096], pB))
            kb.wait("gpsimd", t_qe)
            t_wv = wcol_load(wv1, h * 128, 0)
            kb.wait("tensor", t_wv, t_qe, t_ke)
            t_vmm = None
            for st_ in range(16):
                for dt in range(NT):
                    def fv(e, st_=st_, dt=dt):
                        return e.matmul(
                            pA[:, st_ * 128: st_ * 128 + 128],
                            xt[:, dt * S + st_ * 128: dt * S + st_ * 128 + 128],
                            wb[:, dt * 128: dt * 128 + 128],
                            start=(dt == 0), stop=(dt == NT - 1))
                    if st_ == 15 and dt == NT - 1:
                        t_vmm = kb.sig("tensor", fv)
                    else:
                        kb.op("tensor", fv)
            kb.wait("scalar", t_vmm)

            def fve(e):
                dst = bass.AP(bigB.ap().tensor, 12288,
                              [[14464, 128], [129, 16], [1, 128]])
                srcp = bass.AP(pAh, 0, [[2048, 128], [128, 16], [1, 128]])
                return e.copy(dst, srcp)
            t_ve = kb.sig("scalar", fve)

            tok_sbank = [t_ve, t_ve, t_ve, t_ve]
            tok_avbank = [t_ve] * 4
            tok_avdone = None
            t_p1_prev = None
            for qc in range(4):
                p_toks = {}
                for kt in range(4 * qc + 4):
                    bank = kt % 4
                    kb.wait("tensor", t_ke, t_ve, tok_sbank[bank])

                    def fs(e, kt=kt, qc=qc, bank=bank):
                        return e.matmul(
                            pA[:, bank * 512: bank * 512 + 512],
                            bigB[:, 2048 + kt * 128: 2048 + kt * 128 + 128],
                            bigB[:, qc * 512: qc * 512 + 512],
                            start=True, stop=True)
                    t_smm = kb.sig("tensor", fs)
                    diag = (kt >= 4 * qc)
                    if not diag:
                        kb.wait("scalar", t_smm, tok_avdone)
                        t_p = kb.sig("scalar", lambda e, kt=kt, bank=bank: e.activation(
                            bigB[:, 4096 + kt * 512: 4096 + kt * 512 + 512],
                            pA[:, bank * 512: bank * 512 + 512],
                            ACTF.Exp, scale=SC1))
                        tok_sbank[bank] = t_p
                        p_toks[kt] = t_p
                    else:
                        off = (kt - 4 * qc) * 128
                        kb.wait("scalar", t_smm, t_p1_prev)
                        t_p1 = kb.sig("scalar", lambda e, bank=bank: e.activation(
                            fA[:, 0:512],
                            pA[:, bank * 512: bank * 512 + 512],
                            ACTF.Exp, scale=SC1))
                        tok_sbank[bank] = t_p1
                        kb.wait("vector", t_p1, tok_avdone)
                        t_p2 = kb.sig("vector", lambda e, kt=kt, off=off: e.tensor_tensor(
                            bigB[:, 4096 + kt * 512: 4096 + kt * 512 + 512],
                            fA[:, 0:512],
                            maskw[:, 512 - off: 1024 - off], ALU.mult))
                        p_toks[kt] = t_p2
                        t_p1_prev = t_p2
                for qs in range(4):
                    nkt = 4 * qc + qs + 1
                    bank = qs % 4
                    kb.wait("tensor", tok_avbank[bank],
                            *[p_toks[k] for k in range(nkt)])
                    t_av = None
                    for kt in range(nkt):
                        def fav(e, kt=kt, qc=qc, qs=qs, bank=bank, nkt=nkt):
                            return e.matmul(
                                pB[:, bank * 512: bank * 512 + 129],
                                bigB[:, 4096 + kt * 512 + qs * 128: 4096 + kt * 512 + qs * 128 + 128],
                                bigB[:, 12288 + kt * 129: 12288 + kt * 129 + 129],
                                start=(kt == 0), stop=(kt == nkt - 1))
                        if kt == nkt - 1:
                            t_av = kb.sig("tensor", fav)
                        else:
                            kb.op("tensor", fav)
                    kb.wait("vector", t_av)
                    t_rc = kb.sig("vector", lambda e, bank=bank: e.reciprocal(
                        recb[:, 0:1], pB[:, bank * 512 + 128: bank * 512 + 129]))
                    kb.wait("vector", t_rc)
                    t_an = kb.sig("vector", lambda e, bank=bank, qs=qs: e.tensor_scalar(
                        fB[:, qs * 128: qs * 128 + 128],
                        pB[:, bank * 512: bank * 512 + 128],
                        recb[:, 0:1], None, ALU.mult))
                    kb.wait("tensor", t_an)
                    t_tp = kb.sig("tensor", lambda e, bank=bank, qs=qs: e.transpose(
                        pB[:, bank * 512: bank * 512 + 64].bitcast(BF16),
                        fB[:, qs * 128: qs * 128 + 128], idn[:]))
                    kb.wait("scalar", t_tp)
                    t_ae = kb.sig("scalar", lambda e, h=h, qc=qc, qs=qs, bank=bank: e.copy(
                        bigA[:, h * S + qc * 512 + qs * 128: h * S + qc * 512 + qs * 128 + 128],
                        pB[:, bank * 512: bank * 512 + 64].bitcast(BF16)))
                    tok_avbank[bank] = t_ae
                    tok_avdone = t_av
            kb.barrier()

        utoks = []
        slot_tok = [None, None]
        ps_tok = [None, None]
        for ft in range(NT):
            sl = ft % 2
            kb.wait("gpsimd", slot_tok[sl])
            t_wo = wcol_load(wo1, ft * 128, sl)
            ps = pA if sl == 0 else pB
            t_mm = mm_group(ps,
                            lambda ki, sl=sl: wb[:, sl * 2048 + ki * 128: sl * 2048 + ki * 128 + 128],
                            lambda ki, n0, n1: bigA[:, ki * S + n0: ki * S + n1],
                            NT, S, pe_wait=(t_wo, ps_tok[sl]))
            slot_tok[sl] = t_mm
            tb = fA if sl == 0 else fB
            kb.wait("vector", t_mm)
            t_u1 = kb.sig("vector", lambda e, ps=ps, ft=ft, tb=tb: e.scalar_tensor_tensor(
                tb[:], ps, bo[:, ft:ft + 1], xt[:, ft * S:(ft + 1) * S],
                ALU.add, ALU.add))
            ps_tok[sl] = t_u1
            kb.wait("vector", t_u1)
            t_u2 = kb.sig("vector", lambda e, ft=ft, tb=tb: e.tensor_copy(
                xt[:, ft * S:(ft + 1) * S], tb[:]))
            utoks.append(t_u2)
        return utoks

    def phase4(stop=None):
        tok_pe = emit_stats(lambda t: xt[:, t * S:(t + 1) * S], 2048,
                            pA[0:1, :], pB[0:1, :])
        tok_m, tok_rb = emit_mr_rows_bc(tok_pe, pA[0:1, :], pB[0:1, :],
                                        2048, 2, 2048, pA, pB)
        for t in range(NT):
            tb = fA if t % 2 == 0 else fB
            kb.wait("vector", tok_m, tok_rb)
            t_s = kb.sig("vector", lambda e, t=t, tb=tb: e.tensor_tensor(
                tb[:], xt[:, t * S:(t + 1) * S], pA, ALU.subtract))
            kb.wait("vector", t_s)
            kb.sig("vector", lambda e, t=t, tb=tb: e.tensor_tensor(
                bigA[:, t * S:(t + 1) * S], tb[:], pB, ALU.mult))
        kb.barrier()
        if stop == "p4a":
            return
        t_am = kb.sig("vector", lambda e: e.tensor_scalar(
            bigB[:, 0:2048].bitcast(U16), bigA[:, 0:S].bitcast(U16),
            0x7FFF, None, ALU.bitwise_and))
        acc = 0
        for t in range(1, NT):
            tmp = fA if t % 2 == 0 else fB
            kb.wait("vector", t_am)
            t_a1 = kb.sig("vector", lambda e, t=t, tmp=tmp: e.tensor_scalar(
                tmp.ap().bitcast(U16), bigA[:, t * S:(t + 1) * S].bitcast(U16),
                0x7FFF, None, ALU.bitwise_and))
            kb.wait("vector", t_a1)
            oth = 1 - acc
            t_am = kb.sig("vector", lambda e, acc=acc, oth=oth, tmp=tmp: e.tensor_tensor(
                bigB[:, oth * 2048: oth * 2048 + 2048],
                bigB[:, acc * 2048: acc * 2048 + 2048], tmp[:], ALU.max))
            acc = oth
        curoff = acc * 2048
        if stop == "p4b1":
            return
        kb.wait("tensor", t_am, tok_idn)
        tp_toks = []
        for b in range(16):
            def ftp(e, b=b, curoff=curoff):
                return e.transpose(pA[:, b * 64: b * 64 + 64].bitcast(BF16),
                                   bigB[:, curoff + b * 128: curoff + b * 128 + 128],
                                   idn[:])
            tp_toks.append(kb.sig("tensor", ftp))
        rd_toks = []
        kb.wait("vector", tp_toks[-1])
        for b in range(16):
            rd_toks.append(kb.sig("vector", lambda e, b=b: e.tensor_reduce(
                cmax[:, b:b + 1], pA[:, b * 64: b * 64 + 64].bitcast(BF16),
                AX.X, ALU.max)))
        if stop == "p4b2":
            return
        kb.wait("vector", *rd_toks)
        t1 = kb.sig("vector", lambda e: e.tensor_scalar(
            rsqt[:], cmax[:], 1.0 / 127.0, None, ALU.mult))
        kb.wait("vector", t1)
        t2 = kb.sig("vector", lambda e: e.reciprocal(sqt[:], rsqt[:]))
        row_dm = []
        t_evp = None
        rsq_row = bigb_row(0, 2048)
        t_d1p = None
        for (src, row_i, tk) in ((sqt, 0, t2), (rsqt, 1, t1)):
            kb.wait("tensor", tk, t_evp)
            kb.wait("tensor", tok_idnf)
            t_tp = kb.sig("tensor", lambda e, src=src: e.transpose(
                pBh[0:16, 0:128], src[:], idnf[:]))
            kb.wait("scalar", t_tp, t_d1p)
            t_ev = kb.sig("scalar", lambda e: e.copy(tmp16[:], pBh[0:16, 0:128]))
            t_evp = t_ev
            kb.wait("gpsimd", t_ev)
            t_d1 = kb.sig("gpsimd", lambda e, row_i=row_i: e.dma_start(
                out=bass.AP(row_sp.tensor, row_i * 2048,
                            [[128, 16], [1, 128]]),
                in_=tmp16[:]), dma=True)
            t_d1p = t_d1
            kb.wait("gpsimd", t_d1)
            t_dm = kb.sig("gpsimd", lambda e, row_i=row_i: e.dma_start(
                out=(rows[0:1, :] if row_i == 0 else rsq_row),
                in_=bass.AP(row_sp.tensor, row_i * 2048,
                            [[2048, 1], [1, 2048]])), dma=True)
            row_dm.append(t_dm)
        if stop == "p4b3":
            return
        tok_sq = emit_bcast(rows[0:1, :], pA, 2048, row_dm[0], row_dm[1])
        tok_rsq = emit_bcast(rsq_row, pB, 2048, row_dm[0], row_dm[1])
        kb.wait("scalar", tok_rsq)
        t_rsb = kb.sig("scalar", lambda e: e.copy(rsqb_bf, pB))
        if stop == "p4b4":
            return
        def qfl(nq):
            return bass.AP(bigB.ap().tensor, 4096 + nq * 1024,
                           [[14464, 128], [1, 1024]]).bitcast(F32)

        def uil(nq):
            return bass.AP(bigB.ap().tensor, 8192 + nq * 1024,
                           [[14464, 128], [1, 1024]]).bitcast(I32)

        qtoks = []
        tb_p = [None] * 4
        tc_p = [None] * 4
        td_p = [None] * 4
        for t in range(NT):
            tas = [None] * 4
            for nq in range(4):
                kb.wait("vector", tok_sq, t_rsb, tb_p[nq])
                tas[nq] = kb.sig("vector", lambda e, t=t, nq=nq: e.tensor_tensor(
                    pB[:, nq * 512: nq * 512 + 512],
                    bigA[:, t * S + nq * 512: t * S + nq * 512 + 512],
                    pA[:, nq * 512: nq * 512 + 512], ALU.mult))
            tbs = [None] * 4
            for nq in range(4):
                kb.wait("vector", tas[nq], tc_p[nq])
                tbs[nq] = kb.sig("vector", lambda e, nq=nq: e.tensor_scalar(
                    qfl(nq), pB[:, nq * 512: nq * 512 + 512],
                    -128.0, 127.0, ALU.max, ALU.min))
            tcs = [None] * 4
            for nq in range(4):
                kb.wait("vector", tbs[nq], td_p[nq])
                tcs[nq] = kb.sig("vector", lambda e, nq=nq: e.tensor_copy(
                    uil(nq), qfl(nq)))
            tds = [None] * 4
            for nq in range(4):
                kb.wait("vector", tcs[nq])
                tds[nq] = kb.sig("vector", lambda e, t=t, nq=nq: e.tensor_copy(
                    bigA[:, t * S + nq * 512: t * S + nq * 512 + 512],
                    uil(nq)))
                qtoks.append(tds[nq])
            tb_p = tbs
            tc_p = tcs
            td_p = tds
        if stop == "p4b":
            return
        for t in range(NT):
            for cq in range(4):
                kb.sig("gpsimd", lambda e, t=t, cq=cq: e.dma_start(
                    out=bass.AP(x_sp.tensor, cq * (128 * 16 * 512) + t * 512,
                                [[16 * 512, 128], [1, 512]]),
                    in_=xt[:, t * S + cq * 512: t * S + cq * 512 + 512]),
                    dma=True)
        kb.barrier()

        GS = [0, 10240]
        US = [2048, 12288]
        t_ld_g = [None, None]
        t_ld_u = [None, None]
        tok_st_g = [None, None]
        tok_st_u = [None, None]
        tok_mm_g = [None, None]
        tok_mm_u = [None, None]
        t_sg2 = [None, None]
        t_su2 = [None, None]
        tok_pA = None
        tok_pB = None
        tok_silu = None
        tok_X = [[], []]
        t_p_prev = None
        tok_wdsg = None
        tok_wdsp = None

        def ld_w(ht):
            par = ht % 2
            kb.wait("gpsimd", tok_st_g[par])
            t_ld_g[par] = kb.sig("gpsimd", lambda e, ht=ht, par=par: e.dma_start(
                out=xt[:, par * 2048: par * 2048 + 2048],
                in_=bass.AP(wg1t.tensor, ht * 262144,
                            [[2048, 128], [1, 2048]])), dma=True)
            kb.wait("gpsimd", tok_st_u[par])
            t_ld_u[par] = kb.sig("gpsimd", lambda e, ht=ht, par=par: e.dma_start(
                out=xt[:, 4096 + par * 2048: 4096 + par * 2048 + 2048],
                in_=bass.AP(wu1t.tensor, ht * 262144,
                            [[2048, 128], [1, 2048]])), dma=True)

        def sign_w(ht):
            par = ht % 2
            kb.wait("scalar", t_ld_g[par], tok_mm_g[par])
            t_sg2[par] = kb.sig("scalar", lambda e, par=par: e.sign(
                bigB[:, GS[par]:GS[par] + 2048],
                xt[:, par * 2048: par * 2048 + 2048]))
            tok_st_g[par] = t_sg2[par]
            kb.wait("scalar", t_ld_u[par], tok_mm_u[par])
            t_su2[par] = kb.sig("scalar", lambda e, par=par: e.sign(
                bigB[:, US[par]:US[par] + 2048],
                xt[:, 4096 + par * 2048: 4096 + par * 2048 + 2048]))
            tok_st_u[par] = t_su2[par]

        ld_w(0)
        ld_w(1)
        sign_w(0)
        for ht in range(NH1):
            par = ht % 2
            X = fA if ht % 2 == 0 else fB
            Y = fB if ht % 2 == 0 else fA
            if ht + 2 < NH1:
                ld_w(ht + 2)
            if ht + 1 < NH1:
                sign_w(ht + 1)
            t_gm = mm_group(pA, lambda ki, par=par: bigB[:, GS[par] + ki * 128: GS[par] + ki * 128 + 128],
                            lambda ki, n0, n1: bigA[:, ki * S + n0: ki * S + n1],
                            NT, S, pe_wait=(t_sg2[par], tok_pA))
            tok_mm_g[par] = t_gm
            t_um = mm_group(pB, lambda ki, par=par: bigB[:, US[par] + ki * 128: US[par] + ki * 128 + 128],
                            lambda ki, n0, n1: bigA[:, ki * S + n0: ki * S + n1],
                            NT, S, pe_wait=(t_su2[par], tok_pB))
            tok_mm_u[par] = t_um
            kb.wait("vector", t_gm, t_p_prev, *tok_X[ht % 2])
            t_g1 = kb.sig("vector", lambda e, X=X: e.tensor_tensor(
                X[:], pA, rsqb_bf, ALU.mult))
            tok_pA = t_g1
            kb.wait("scalar", t_g1, tok_silu)
            t_g2 = kb.sig("scalar", lambda e, X=X: e.activation(
                bigB[:, 4096:6144], X[:], ACTF.Silu))
            kb.wait("vector", t_um, *tok_X[(ht + 1) % 2])
            t_u1 = kb.sig("vector", lambda e, Y=Y: e.tensor_tensor(
                Y[:], pB, rsqb_bf, ALU.mult))
            tok_pB = t_u1
            kb.wait("vector", t_g2, t_u1)
            t_p = kb.sig("vector", lambda e, X=X, Y=Y: e.tensor_tensor(
                X[:], bigB[:, 4096:6144], Y[:], ALU.mult))
            tok_silu = t_p
            t_p_prev = t_p
            kb.wait("scalar", t_p)
            t_ps_l = []
            for cq in range(4):
                t_ps_l.append(kb.sig("scalar", lambda e, ht=ht, X=X, cq=cq: e.dma_start(
                    out=bass.AP(p_sp.tensor, cq * (128 * 64 * 512) + ht * 512,
                                [[64 * 512, 128], [1, 512]]),
                    in_=X[:, cq * 512: cq * 512 + 512]), dma=True))
            tok_X[ht % 2] = t_ps_l
            kb.wait("gpsimd", tok_wdsg)
            t_w = kb.sig("gpsimd", lambda e, ht=ht: e.dma_start(
                out=bigB[:, 6144:8192],
                in_=bass.AP(wd1t.tensor, ht * 128 * 2048,
                            [[2048, 128], [1, 2048]])), dma=True)
            kb.wait("scalar", t_w, tok_wdsp)
            t_wsg = kb.sig("scalar", lambda e: e.sign(
                bigB[:, 8192:10240], bigB[:, 6144:8192]))
            tok_wdsg = t_wsg
            kb.wait("sync", t_wsg)

            def fwdsp(e, ht=ht):
                dst = bass.AP(wds_sp.tensor,
                              (ht // 16) * 128 * 2048 + (ht % 16) * 128,
                              [[2048, 128], [4 * 128 * 2048, 16], [1, 128]])
                srcp = bass.AP(bigB.ap().tensor, 8192,
                               [[14464, 128], [128, 16], [1, 128]])
                return e.dma_start(out=dst, in_=srcp)
            tok_wdsp = kb.sig("sync", fwdsp, dma=True)
        kb.barrier()
        if stop == "p4c":
            return
        kb.barrier()
        if stop == "p4d":
            return
        pc_toks = [None] * 4
        xr_toks = [None] * 4

        def ld_pc(ch, *wait):
            kb.wait("sync", *wait)
            pc_toks[ch] = kb.sig("sync", lambda e, ch=ch: e.dma_start(
                out=xt[:, 0:32768],
                in_=bass.AP(p_sp.tensor, ch * (128 * 64 * 512),
                            [[64 * 512, 128], [1, 64 * 512]])), dma=True)

        def ld_xr(ch, *wait):
            kb.wait("gpsimd", *wait)
            xr_toks[ch] = kb.sig("gpsimd", lambda e, ch=ch: e.dma_start(
                out=bass.AP(bigA.ap().tensor, 8192,
                            [[32768, 128], [1, 8192]]),
                in_=bass.AP(x_sp.tensor, ch * (128 * 16 * 512),
                            [[16 * 512, 128], [1, 16 * 512]])), dma=True)

        ld_pc(0)
        ld_xr(0)
        for ch in range(4):
            c0 = ch * 512
            t_pc = pc_toks[ch]
            t_xr = xr_toks[ch]
            tok_wsl = [None, None]
            tok_yb = [None] * 4
            for dm in range(16):
                bank = dm % 4
                mml = None
                for wv in range(4):
                    sl = wv % 2
                    kb.wait("sync", tok_wsl[sl])
                    t_l = kb.sig("sync", lambda e, dm=dm, wv=wv, sl=sl: e.dma_start(
                        out=bass.AP(wb.ap().tensor, sl * 2048,
                                    [[4096, 128], [1, 2048]]),
                        in_=bass.AP(wds_sp.tensor, (dm * 4 + wv) * 128 * 2048,
                                    [[2048, 128], [1, 2048]])),
                        dma=True)
                    kb.wait("tensor", t_l, t_pc,
                            tok_yb[bank] if wv == 0 else None)
                    for ki in range(16):
                        st = (wv == 0 and ki == 0)
                        sp = (wv == 3 and ki == 15)

                        def f(e, ki=ki, wv=wv, sl=sl, bank=bank, st=st, sp=sp):
                            return e.matmul(
                                pA[:, bank * 512: bank * 512 + 512],
                                wb[:, sl * 2048 + ki * 128: sl * 2048 + ki * 128 + 128],
                                xt[:, (wv * 16 + ki) * 512: (wv * 16 + ki) * 512 + 512],
                                start=st, stop=sp)
                        if ki == 15:
                            mml = kb.sig("tensor", f)
                        else:
                            kb.op("tensor", f)
                    tok_wsl[sl] = mml
                kb.wait("scalar", mml)
                t_ybf = kb.sig("scalar", lambda e, dm=dm, bank=bank: e.copy(
                    bigA[:, dm * 512: dm * 512 + 512],
                    pA[:, bank * 512: bank * 512 + 512]))
                tok_yb[bank] = t_ybf
            if ch + 1 < 4:
                ld_pc(ch + 1, mml)
            utoks = []
            tprev2 = None
            for dm in range(16):
                tb = fA if dm % 2 == 0 else fB
                kb.wait("vector", tok_yb[dm % 4], t_xr, tprev2)
                tu1 = kb.sig("vector", lambda e, dm=dm, tb=tb: e.tensor_tensor(
                    tb[:, 0:512], bigA[:, dm * 512: dm * 512 + 512],
                    bigA[:, 8192 + dm * 512: 8192 + dm * 512 + 512], ALU.add))
                kb.wait("vector", tu1)
                tu2 = kb.sig("vector", lambda e, dm=dm, tb=tb: e.tensor_copy(
                    bigA[:, dm * 512: dm * 512 + 512], tb[:, 0:512]))
                utoks.append(tu2)
                tprev2 = tu2
            if ch + 1 < 4:
                ld_xr(ch + 1, utoks[-1])
            tok_pe = emit_stats(
                lambda t: bigA[:, t * 512: t * 512 + 512], 512,
                pA[0:1, 0:512], pA[0:1, 512:1024], utoks=utoks)
            tok_m, tok_rb2 = emit_mr_rows_bc(
                tok_pe, pA[0:1, 0:512], pA[0:1, 512:1024], 512, 0, 2048,
                pA[:, 1024:1536], pA[:, 1536:2048])
            def ysl(sl):
                tb_ = fA if sl < 2 else fB
                return bass.AP(tb_.ap().tensor, (sl % 2) * 1024,
                               [[2048, 128], [1, 1024]]).bitcast(F32)

            tokdm = [None] * 4
            tsub_p = [None] * 4
            for t in range(NT):
                sl = t % 4
                kb.wait("vector", tok_m, tok_rb2, tsub_p[sl])
                t_s = kb.sig("vector", lambda e, t=t, sl=sl: e.tensor_tensor(
                    qfl(sl), bigA[:, t * 512: t * 512 + 512],
                    pA[:, 1024:1536], ALU.subtract))
                kb.wait("vector", t_s, tokdm[sl])
                t_n = kb.sig("vector", lambda e, sl=sl: e.tensor_tensor(
                    ysl(sl), qfl(sl), pA[:, 1536:2048], ALU.mult))
                tsub_p[sl] = t_n
                kb.wait("sync", t_n)
                tokdm[sl] = kb.sig("sync", lambda e, t=t, c0=c0, sl=sl: e.dma_start(
                    out=bass.AP(y_t.tensor, t * 128 * S + c0,
                                [[S, 128], [1, 512]]),
                    in_=ysl(sl)), dma=True)
            kb.barrier()

    _gen = dwn_gen()
    _gen_done = [False]

    def pump(n=1):
        for _ in range(n):
            if _gen_done[0]:
                return
            try:
                next(_gen)
            except StopIteration:
                _gen_done[0] = True
                return

    utoks1 = phase1(pump)
    while not _gen_done[0]:
        pump(10)
    kb.barrier()
    norm1 = emit_full_ln(utoks1, 0)
    kb.barrier()
    if stop != "p1":
        phase2a(tok_idn)
        kb.barrier()
        phase2b()
        if stop != "p2":
            utoks3 = phase3()
            norm3 = emit_full_ln(utoks3, 0)
            kb.barrier()
            if stop != "p3":
                phase4(stop)

    if stop is not None and stop != "full":
        kb.barrier()
        tokdmp = None
        for t in range(NT):
            for qd in range(4):
                kb.wait("vector", tokdmp)
                tk = kb.sig("vector", lambda e, t=t, qd=qd: e.tensor_copy(
                    qf[:], xt[:, t * S + qd * 512: t * S + qd * 512 + 512]))
                kb.wait("sync", tk)
                tokdmp = kb.sig("sync", lambda e, t=t, qd=qd: e.dma_start(
                    out=bass.AP(y_t.tensor, t * 128 * S + qd * 512,
                                [[S, 128], [1, 512]]),
                    in_=qf[:]), dma=True)

    for q in ("q_sp", "q_sc"):
        for r in range(KB.NQROT):
            nm = f"{q}{r}"
            if kb.cnt[nm] > 0:
                kb.wait("sync", (nm, kb.cnt[nm]))
    for r in range(KB.NQROT):
        nm = f"q_gp{r}"
        if kb.cnt[nm] > 0:
            kb.wait("gpsimd", (nm, kb.cnt[nm]))

    with nc.Block() as block:
        @block.sync
        def _(e):
            for f in kb.th["sync"]:
                f(e)

        @block.scalar
        def _(e):
            for f in kb.th["scalar"]:
                f(e)

        @block.vector
        def _(e):
            for f in kb.th["vector"]:
                f(e)

        @block.gpsimd
        def _(e):
            for f in kb.th["gpsimd"]:
                f(e)

        @block.tensor
        def _(e):
            for f in kb.th["tensor"]:
                f(e)

    es.__exit__(None, None, None)
    return nc


def _host_inputs(inputs):
    x = np.asarray(inputs["L0_attn_x"])
    base = {
        "wq0t": np.ascontiguousarray(np.asarray(inputs["L0_attn_Wq"]).T),
        "wk0t": np.ascontiguousarray(np.asarray(inputs["L0_attn_Wk"]).T),
        "wv0t": np.ascontiguousarray(np.asarray(inputs["L0_attn_Wv"]).T),
        "wo0t": np.ascontiguousarray(np.asarray(inputs["L0_attn_Wo"]).T),
        "gup_pk": np.ascontiguousarray(np.asarray(inputs["L0_ffn_gate_up_packed"])),
        "gup_sc": np.ascontiguousarray(np.asarray(inputs["L0_ffn_gate_up_scales"]).T),
        "dwn_pk": np.ascontiguousarray(np.asarray(inputs["L0_ffn_down_packed"])),
        "dwn_sc": np.ascontiguousarray(np.asarray(inputs["L0_ffn_down_scales"]).T),
        "wq1": np.ascontiguousarray(np.asarray(inputs["L1_attn_Wq"])),
        "wk1": np.ascontiguousarray(np.asarray(inputs["L1_attn_Wk"])),
        "wv1": np.ascontiguousarray(np.asarray(inputs["L1_attn_Wv"])),
        "wo1": np.ascontiguousarray(np.asarray(inputs["L1_attn_Wo"])),
        "bo1": np.ascontiguousarray(np.asarray(inputs["L1_attn_bo"]).reshape(16, 128).T),
        "wg1t": np.ascontiguousarray(
            np.asarray(inputs["L1_ffn_W_g"]).reshape(64, 128, 16, 128)
            .transpose(0, 3, 2, 1).reshape(8192, 2048)),
        "wu1t": np.ascontiguousarray(
            np.asarray(inputs["L1_ffn_W_u"]).reshape(64, 128, 16, 128)
            .transpose(0, 3, 2, 1).reshape(8192, 2048)),
        "wd1t": np.ascontiguousarray(np.asarray(inputs["L1_ffn_W_d"]).T),
        "idn": np.eye(128, dtype=ml_dtypes.bfloat16),
        "idnf": np.eye(128, dtype=np.float32),
        "maskw": (np.arange(1024)[None, :] >= (np.arange(128)[:, None] + 512)
                  ).astype(ml_dtypes.bfloat16),
    }
    in_maps = []
    for c in range(8):
        m = dict(base)
        m["x_t"] = np.ascontiguousarray(np.asarray(x[c]).T)
        in_maps.append(m)
    return in_maps


def kernel(**inputs):
    in_maps = _host_inputs(inputs)
    nc = build_program()
    res = run_bass_kernel_spmd(nc, in_maps, core_ids=list(range(8)))
    out = np.stack([np.asarray(r["y_t"]).T for r in res.results], axis=0)
    return np.ascontiguousarray(out.astype(np.float32))


# version 09 (final)
# speedup vs baseline: 1.2357x; 1.2357x over previous
import numpy as np
import ml_dtypes
from contextlib import ExitStack

import concourse.bass as bass
import concourse.mybir as mybir
from concourse.bass_utils import run_bass_kernel_spmd

F32 = mybir.dt.float32
BF16 = mybir.dt.bfloat16
F16 = mybir.dt.float16
I32 = mybir.dt.int32
U16 = mybir.dt.uint16
ALU = mybir.AluOpType
ACTF = mybir.ActivationFunctionType
AX = mybir.AxisListType

D = 2048
S = 2048
NT = 16
I0 = 5632
NI0 = 44
H1 = 8192
NH1 = 64
DENOM0 = float(np.sqrt(np.float32(2048.0))) / 2048.0
SC1 = 1.0 / float(np.sqrt(np.float32(128.0)))
LNEPS = 1e-5
RMS0EPS = 1e-6
RMS1EPS = 1e-8

ENGS = ("sync", "scalar", "vector", "gpsimd", "tensor")


class KB:
    _DSEM = {"sync": "d_sp", "scalar": "d_sc", "vector": "d_ve",
             "gpsimd": "d_gp", "tensor": "d_pe"}
    _QSEM = {"sync": "q_sp", "scalar": "q_sc", "gpsimd": "q_gp"}

    NQROT = 8

    def __init__(self, nc, es):
        self.nc = nc
        self.th = {e: [] for e in ENGS}
        self.semh = {}
        self.cnt = {}
        self.waited = {e: {} for e in ENGS}
        self.qrot = {}
        names = ["d_sp", "d_sc", "d_ve", "d_gp", "d_pe"]
        for q in ("q_sp", "q_sc", "q_gp"):
            self.qrot[q] = 0
            for r in range(self.NQROT):
                names.append(f"{q}{r}")
        for name in names:
            self.semh[name] = es.enter_context(nc.semaphore(name))
            self.cnt[name] = 0

    def op(self, eng, fn):
        self.th[eng].append(fn)

    def sig(self, eng, fn, dma=False):
        if dma:
            q = self._QSEM[eng]
            r = self.qrot[q] % self.NQROT
            self.qrot[q] += 1
            name = f"{q}{r}"
            amt = 16
            prev = self.cnt[name]
            if prev > 0:
                self.wait(eng, (name, prev))
        else:
            name = self._DSEM[eng]
            amt = 1
        self.cnt[name] += amt
        v = self.cnt[name]
        sem = self.semh[name]

        def fn2(e, fn=fn, sem=sem, amt=amt):
            fn(e).then_inc(sem, amt)
        self.th[eng].append(fn2)
        return (name, v)

    def wait(self, eng, *toks):
        best = {}
        for t in toks:
            if t is None:
                continue
            nm, v = t
            if v > best.get(nm, -1):
                best[nm] = v
        for nm, v in sorted(best.items()):
            if self.waited[eng].get(nm, 0) >= v:
                continue
            self.waited[eng][nm] = v
            sem = self.semh[nm]
            self.th[eng].append(lambda e, sem=sem, v=v: e.wait_ge(sem, v))

    def barrier(self):
        toks = []
        for e in ENGS:
            toks.append(self.sig(e, lambda x: x.drain(fusable=False)))
        for q in ("q_sp", "q_sc", "q_gp"):
            for r in range(self.NQROT):
                nm = f"{q}{r}"
                if self.cnt[nm] > 0:
                    toks.append((nm, self.cnt[nm]))
        for e in ENGS:
            self.wait(e, *toks)


def build_program(stop=None):
    nc = bass.Bass("TRN2", target_bir_lowering=False)
    es = ExitStack()
    es.__enter__()

    def din(name, shape, dt):
        return nc.dram_tensor(name, shape, dt, kind="ExternalInput").ap()

    def dout(name, shape, dt):
        return nc.dram_tensor(name, shape, dt, kind="ExternalOutput").ap()

    def dscr(name, shape, dt):
        return nc.dram_tensor(name, shape, dt).ap()

    x_t = din("x_t", [D, S], F32)
    wq0t = din("wq0t", [D, D], F32)
    wk0t = din("wk0t", [D, D], F32)
    wv0t = din("wv0t", [D, D], F32)
    wo0t = din("wo0t", [D, D], F32)
    gup_pk = din("gup_pk", [2 * I0, 256], I32)
    gup_sc = din("gup_sc", [2 * I0, 16], F16)
    dwn_pk = din("dwn_pk", [D, 704], I32)
    dwn_sc = din("dwn_sc", [D, 44], F16)
    wq1 = din("wq1", [D, D], F32)
    wk1 = din("wk1", [D, D], F32)
    wv1 = din("wv1", [D, D], F32)
    wo1 = din("wo1", [D, D], F32)
    bo1 = din("bo1", [128, 16], F32)
    wg1t = din("wg1t", [H1, D], F32)
    wu1t = din("wu1t", [H1, D], F32)
    wd1t = din("wd1t", [H1, D], F32)
    idn_i = din("idn", [128, 128], BF16)
    idnf_i = din("idnf", [128, 128], F32)
    maskw_i = din("maskw", [128, 1024], BF16)

    y_t = dout("y_t", [D, S], F32)

    wgu_sp = dscr("wgu_sp", [88 * 16, 16384], BF16)
    wdn_sp = dscr("wdn_sp", [16 * 44, 16384], BF16)
    wdn_nat = dscr("wdn_nat", [16, 128 * 5632], BF16)
    h_sp = dscr("h_sp", [NI0, 128 * S], BF16)
    p_sp = dscr("p_sp", [NH1, 128 * S], BF16)
    wds_sp = dscr("wds_sp", [16 * 64, 16384], BF16)
    x_sp = dscr("x_sp", [NT, 128 * S], BF16)
    row_sp = dscr("row_sp", [2, 2048], F32)

    def sb(name, shape, dt):
        return es.enter_context(nc.sbuf_tensor(name, shape, dt))

    xt = sb("xt", [128, 32768], BF16)
    bigA = sb("bigA", [128, 32768], BF16)
    bigB = sb("bigB", [128, 14464], BF16)
    wb = sb("wb", [128, 4096], BF16)
    fA = sb("fA", [128, 2048], BF16)
    fB = sb("fB", [128, 2048], BF16)
    rows = sb("rows", [64, 2048], F32)
    pkbuf = sb("pkbuf", [128, 1024], I32)
    u_i = sb("u_i", [128, 768], I32)
    sg_i = sb("sg_i", [128, 768], I32)
    sg_u = sb("sg_u", [128, 768], U16)
    f_b = sb("f_b", [128, 768], BF16)
    d1_b = sb("d1_b", [128, 768], BF16)
    d2_b = sb("d2_b", [128, 768], BF16)
    scb = sb("scb", [128, 88], F32)
    idn = sb("idn_s", [128, 128], BF16)
    idnf = sb("idnf_s", [128, 128], F32)
    maskw = sb("maskw_s", [128, 1024], BF16)
    bo = sb("bo_s", [128, 16], F32)
    onescol = sb("onescol", [128, 1], BF16)
    onesrow = sb("onesrow", [64, 128], F32)
    cmax = sb("cmax", [128, 16], F32)
    sqt = sb("sqt", [128, 16], F32)
    rsqt = sb("rsqt", [128, 16], F32)
    tmp16 = sb("tmp16", [16, 128], F32)
    recb = sb("recb", [128, 8], F32)
    qf = sb("qf", [128, 512], F32)
    epsb = sb("epsb", [128, 4], F32)

    pAh = es.enter_context(nc.psum_tensor("pA", [128, 2048], F32))
    pBh = es.enter_context(nc.psum_tensor("pB", [128, 2048], F32))
    pA = pAh[:, :]
    pB = pBh[:, :]

    kb = KB(nc, es)

    rsqb_bf = bass.AP(pkbuf.ap().tensor, 0,
                      [[1024, 128], [1, 1024]]).bitcast(BF16)
    qfb = bass.AP(u_i.ap().tensor, 0, [[768, 128], [1, 512]]).bitcast(F32)

    def bigb_row(c0, w):
        return bass.AP(bigB.ap().tensor, c0, [[14464, 1], [1, 2 * w]]).bitcast(F32)

    kb.sig("vector", lambda e: e.memset(onescol[:], 1.0))
    kb.sig("vector", lambda e: e.memset(onesrow[0:1, :], 1.0))
    kb.sig("vector", lambda e: e.memset(onesrow[32:33, :], 1.0))

    def set_vones(e):
        ins = None
        for t in range(16):
            ins = e.memset(bigB[:, 12288 + t * 129 + 128: 12288 + t * 129 + 129], 1.0)
        return ins
    kb.sig("vector", set_vones)
    kb.sig("vector", lambda e: e.memset(epsb[:, 0:1], LNEPS))
    kb.sig("vector", lambda e: e.memset(epsb[:, 1:2], RMS0EPS))
    kb.sig("vector", lambda e: e.memset(epsb[:, 2:3], RMS1EPS))
    tok_idn = kb.sig("sync", lambda e: e.dma_start(out=idn[:], in_=idn_i), dma=True)
    tok_idnf = kb.sig("sync", lambda e: e.dma_start(out=idnf[:], in_=idnf_i), dma=True)
    kb.sig("sync", lambda e: e.dma_start(out=maskw[:], in_=maskw_i), dma=True)
    kb.sig("sync", lambda e: e.dma_start(out=bo[:], in_=bo1), dma=True)
    for t in range(NT):
        kb.sig("gpsimd", lambda e, t=t: e.dma_start(
            out=xt[:, t * S:(t + 1) * S],
            in_=x_t[t * 128:(t + 1) * 128, :]), dma=True)
    kb.barrier()

    def cast_load(dst_ap, src_ap):
        return kb.sig("gpsimd", lambda e: e.dma_start(out=dst_ap, in_=src_ap),
                      dma=True)

    def wcol_load(wt, c0, slot):
        dst = bass.AP(wb.ap().tensor, slot * 2048,
                      [[4096, 128], [128, 16], [1, 128]])
        src = bass.AP(wt.tensor, (c0 // 128) * 262144,
                      [[2048, 128], [1, 2048]])
        return cast_load(dst, src)

    def wrow_load(wt, r0, slot):
        rl = wt.tensor.shape[1]
        dst = wb[:, slot * 2048: slot * 2048 + 2048]
        src = bass.AP(wt.tensor, r0 * rl, [[rl, 128], [1, 2048]])
        return cast_load(dst, src)

    def mm_group(out_ps, lhsT_of, rhs_of, ksteps, nw, pe_wait=()):
        kb.wait("tensor", *pe_wait)
        nch = [(n, min(n + 512, nw)) for n in range(0, nw, 512)]
        last = None
        for ki in range(ksteps):
            for (n0, n1) in nch:
                st = (ki == 0)
                sp = (ki == ksteps - 1)

                def f(e, ki=ki, n0=n0, n1=n1, st=st, sp=sp):
                    return e.matmul(out_ps[:, n0:n1], lhsT_of(ki),
                                    rhs_of(ki, n0, n1), start=st, stop=sp)
                if sp and n1 == nw:
                    last = kb.sig("tensor", f)
                else:
                    kb.op("tensor", f)
        return last

    def emit_stats(get_u, w, sum_ps, ssq_ps, utoks=None, nt=NT):
        tokpe = None
        if utoks is not None:
            kb.wait("vector", *utoks)
            kb.wait("tensor", *utoks)
        sq_rd = [None, None]
        for t in range(nt):
            sq = fA if t % 2 == 0 else fB
            kb.wait("vector", sq_rd[t % 2])
            tsq = kb.sig("vector", lambda e, t=t, sq=sq, w=w: e.tensor_tensor(
                sq[:, 0:w], get_u(t), get_u(t), ALU.mult))
            for (n0, n1) in [(n, min(n + 512, w)) for n in range(0, w, 512)]:
                def f1(e, t=t, n0=n0, n1=n1):
                    return e.matmul(sum_ps[:, n0:n1], onescol[:],
                                    get_u(t)[:, n0:n1], start=(t == 0),
                                    stop=(t == nt - 1))
                kb.op("tensor", f1)
            kb.wait("tensor", tsq)
            for (n0, n1) in [(n, min(n + 512, w)) for n in range(0, w, 512)]:
                def f2(e, t=t, n0=n0, n1=n1, sq=sq):
                    return e.matmul(ssq_ps[:, n0:n1], onescol[:],
                                    sq[:, n0:n1], start=(t == 0),
                                    stop=(t == nt - 1))
                tk = kb.sig("tensor", f2)
                if t == nt - 1 and n1 == w:
                    tokpe = tk
            sq_rd[t % 2] = tk
        return tokpe

    def emit_mr_rows_bc(tok_pe, sum_ps, ssq_ps, w, epsi, nfeat, m_ps, r_ps):
        scrA = bigb_row(10240, 512)
        scrB = bigb_row(11264, 512)
        kb.wait("vector", tok_pe)
        t1 = kb.sig("vector", lambda e: e.tensor_scalar(
            rows[0:1, 0:w], sum_ps[:, 0:w], 1.0 / nfeat, None, ALU.mult))
        tok_m = emit_bcast(rows[0:1, 0:w], m_ps, w, t1)
        tok_r = None
        t5 = None
        for c0 in range(0, w, 512):
            kb.wait("vector", t1, t5)
            t2 = kb.sig("vector", lambda e, c0=c0: e.tensor_tensor(
                scrA, rows[0:1, c0:c0 + 512], rows[0:1, c0:c0 + 512], ALU.mult))
            kb.wait("vector", t2, tok_r)
            t3 = kb.sig("vector", lambda e, c0=c0: e.scalar_tensor_tensor(
                scrB, ssq_ps[:, c0:c0 + 512], 1.0 / nfeat, scrA,
                ALU.mult, ALU.subtract))
            kb.wait("scalar", t3)
            t4 = kb.sig("scalar", lambda e: e.activation(
                scrA, scrB, ACTF.Ln, bias=epsb[0:1, epsi:epsi + 1]))
            kb.wait("scalar", t4)
            t5 = kb.sig("scalar", lambda e: e.activation(
                scrB, scrA, ACTF.Exp, scale=-0.5))
            rp = r_ps[:, c0:c0 + 512] if w > 512 else r_ps
            tok_r = emit_bcast(scrB, rp, 512, t5, tok_r)
        return tok_m, tok_r

    def emit_bcast(row_ap, out_ps, w, *pe_wait):
        kb.wait("tensor", *pe_wait)
        bp = row_ap.base_partition()
        ones = onesrow[bp:bp + 1, :]
        last = None
        for (n0, n1) in [(n, min(n + 512, w)) for n in range(0, w, 512)]:
            def f(e, n0=n0, n1=n1, ones=ones):
                return e.matmul(out_ps[:, n0:n1], ones,
                                row_ap[:, n0:n1], start=True, stop=True)
            if n1 == w:
                last = kb.sig("tensor", f)
            else:
                kb.op("tensor", f)
        return last

    def phase1(pump):
        tok_pA = None
        tok_pB = None
        tok_s0 = None
        tok_s1 = None
        t_vh = None
        t_scan_prev = None
        for et in range(NT):
            pump(6)
            c0 = et * 128
            kb.wait("gpsimd", tok_s0)
            t_wk = wcol_load(wk0t, c0, 0)
            kb.wait("gpsimd", tok_s1)
            t_wv = wcol_load(wv0t, c0, 1)

            def lh(ki, sl):
                return wb[:, sl * 2048 + ki * 128: sl * 2048 + ki * 128 + 128]

            def rh(ki, n0, n1):
                return xt[:, ki * S + n0: ki * S + n1]

            t_mmK = mm_group(pA, lambda ki: lh(ki, 0), rh, NT, S,
                             pe_wait=(t_wk, tok_pA))
            kb.wait("scalar", t_mmK, t_vh)
            t_k = kb.sig("scalar", lambda e: e.copy(fA[:], pA))
            tok_s0 = t_mmK
            kb.wait("gpsimd", t_mmK)
            t_wq = wcol_load(wq0t, c0, 0)
            t_mmV = mm_group(pB, lambda ki: lh(ki, 1), rh, NT, S,
                             pe_wait=(t_wv, tok_pB))
            tok_s1 = t_mmV
            kb.wait("vector", t_mmV, t_k, t_scan_prev)
            t_kv = kb.sig("vector", lambda e: e.tensor_tensor(
                fB[:], pB, fA[:], ALU.mult))
            tok_pB = t_kv
            kb.wait("vector", t_kv)
            t_scan = kb.sig("vector", lambda e: e.tensor_tensor_scan(
                fA[:], fB[:], fB[:], 0.0, ALU.add, ALU.bypass))
            t_scan_prev = t_scan
            t_mmQ = mm_group(pA, lambda ki: lh(ki, 0), rh, NT, S,
                             pe_wait=(t_wq, t_k))
            tok_s0 = t_mmQ
            kb.wait("vector", t_scan, t_mmQ)
            t_vh = kb.sig("vector", lambda e, et=et: e.scalar_tensor_tensor(
                bigA[:, et * S:(et + 1) * S], fA[:], DENOM0, pA,
                ALU.mult, ALU.mult))
            tok_pA = t_vh

        utoks = []
        ps_tok = [tok_pA, tok_pB]
        slot_tok = [tok_s0, tok_s1]
        for ft in range(NT):
            pump(6)
            sl = ft % 2
            kb.wait("gpsimd", slot_tok[sl])
            t_wo = wcol_load(wo0t, ft * 128, sl)
            ps = pA if sl == 0 else pB
            t_mm = mm_group(ps,
                            lambda ki, sl=sl: wb[:, sl * 2048 + ki * 128: sl * 2048 + ki * 128 + 128],
                            lambda ki, n0, n1: bigA[:, ki * S + n0: ki * S + n1],
                            NT, S, pe_wait=(t_wo, t_vh, ps_tok[sl]))
            slot_tok[sl] = t_mm
            tb = fA if sl == 0 else fB
            kb.wait("vector", t_mm)
            t_u1 = kb.sig("vector", lambda e, ps=ps, ft=ft, tb=tb: e.tensor_tensor(
                tb[:], ps, xt[:, ft * S:(ft + 1) * S], ALU.add))
            ps_tok[sl] = t_u1
            kb.wait("vector", t_u1)
            t_u2 = kb.sig("vector", lambda e, ft=ft, tb=tb: e.tensor_copy(
                xt[:, ft * S:(ft + 1) * S], tb[:]))
            utoks.append(t_u2)
        return utoks

    def emit_full_ln(utoks, epsi):
        tok_pe = emit_stats(lambda t: xt[:, t * S:(t + 1) * S], 2048,
                            pA[0:1, :], pB[0:1, :], utoks=utoks)
        tok_m, tok_rb = emit_mr_rows_bc(tok_pe, pA[0:1, :], pB[0:1, :],
                                        2048, epsi, 2048, pA, pB)
        ntoks = []
        for t in range(NT):
            tb = fA if t % 2 == 0 else fB
            kb.wait("vector", tok_m, tok_rb)
            t_s = kb.sig("vector", lambda e, t=t, tb=tb: e.tensor_tensor(
                tb[:], xt[:, t * S:(t + 1) * S], pA, ALU.subtract))
            kb.wait("vector", t_s)
            t_n = kb.sig("vector", lambda e, t=t, tb=tb: e.tensor_tensor(
                xt[:, t * S:(t + 1) * S], tb[:], pB, ALU.mult))
            ntoks.append(t_n)
        return ntoks

    dec_lane_or = [None, None, None, None]
    dec_lane_st2 = [None, None, None, None]
    dec_lane_st3 = [None, None, None, None]
    dec_lane_st6 = [None, None, None, None]
    dec_asm_free = {0: None, 2048: None, 4096: None}

    def emit_decode(t_pk, pko, C, FD, G, ng, sco, asm0, scaled_t, scaled0,
                    tok_scaled_free, t_sc):
        units = []
        for j in range(8):
            for hf in range(C // FD):
                units.append((j, hf * FD))
        n = len(units)
        st_tok = [[None] * 8 for _ in range(G)]
        last_or = None
        for w0 in range(0, n, G):
            wave = units[w0:w0 + G]
            for l, (j, co) in enumerate(wave):
                lo = l * FD
                kb.wait("vector", dec_lane_or[l], dec_lane_st3[l], t_pk,
                        dec_asm_free[asm0])
                st_tok[l][0] = kb.sig("vector", lambda e, j=j, co=co, lo=lo, pko=pko, FD=FD: e.tensor_scalar(
                    u_i[:, lo:lo + FD], pkbuf[:, pko + co:pko + co + FD],
                    4 * j, 7, ALU.logical_shift_right, ALU.bitwise_and))
            for l, (j, co) in enumerate(wave):
                lo = l * FD
                sbit = 4 * j + 3
                msk = (1 << sbit) if sbit < 31 else -(1 << 31)
                kb.wait("vector", dec_lane_st2[l])
                if sbit <= 15:
                    st_tok[l][1] = kb.sig("vector", lambda e, co=co, lo=lo, pko=pko, FD=FD, msk=msk, sh=15 - sbit: e.tensor_scalar(
                        sg_i[:, lo:lo + FD], pkbuf[:, pko + co:pko + co + FD],
                        msk, sh, ALU.bitwise_and, ALU.logical_shift_left))
                else:
                    st_tok[l][1] = kb.sig("vector", lambda e, co=co, lo=lo, pko=pko, FD=FD, msk=msk, sh=sbit - 15: e.tensor_scalar(
                        sg_i[:, lo:lo + FD], pkbuf[:, pko + co:pko + co + FD],
                        msk, sh, ALU.bitwise_and, ALU.logical_shift_right))
            for l in range(len(wave)):
                lo = l * FD
                kb.wait("vector", st_tok[l][1], dec_lane_or[l])
                st_tok[l][2] = kb.sig("vector", lambda e, lo=lo, FD=FD: e.tensor_copy(
                    sg_u[:, lo:lo + FD], sg_i[:, lo:lo + FD]))
                dec_lane_st2[l] = st_tok[l][2]
            for l in range(len(wave)):
                lo = l * FD
                kb.wait("vector", st_tok[l][0], dec_lane_st6[l])
                st_tok[l][3] = kb.sig("vector", lambda e, lo=lo, FD=FD: e.tensor_copy(
                    f_b[:, lo:lo + FD], u_i[:, lo:lo + FD]))
                dec_lane_st3[l] = st_tok[l][3]
            for l in range(len(wave)):
                lo = l * FD
                kb.wait("vector", st_tok[l][3])
                st_tok[l][4] = kb.sig("vector", lambda e, lo=lo, FD=FD: e.tensor_scalar(
                    d1_b[:, lo:lo + FD], f_b[:, lo:lo + FD], 2.0, 8.0,
                    ALU.mult, ALU.subtract))
            for l in range(len(wave)):
                lo = l * FD
                kb.wait("vector", st_tok[l][4])
                st_tok[l][5] = kb.sig("vector", lambda e, lo=lo, FD=FD: e.scalar_tensor_tensor(
                    d2_b[:, lo:lo + FD], f_b[:, lo:lo + FD], 2.0,
                    d1_b[:, lo:lo + FD], ALU.subtract, ALU.max))
            for l in range(len(wave)):
                lo = l * FD
                kb.wait("vector", st_tok[l][5])
                st_tok[l][6] = kb.sig("vector", lambda e, lo=lo, FD=FD: e.scalar_tensor_tensor(
                    d1_b[:, lo:lo + FD], f_b[:, lo:lo + FD], 0.5,
                    d2_b[:, lo:lo + FD], ALU.mult, ALU.max))
                dec_lane_st6[l] = st_tok[l][6]
            for l, (j, co) in enumerate(wave):
                lo = l * FD
                kb.wait("vector", st_tok[l][6], st_tok[l][2])

                def orw(e, asm0=asm0, co=co, j=j, FD=FD, lo=lo):
                    outap = bass.AP(bigB.ap().tensor, asm0 + 8 * co + j,
                                    [[14464, 128], [8, FD]]).bitcast(U16)
                    return e.tensor_tensor(
                        outap, d1_b[:, lo:lo + FD].bitcast(U16),
                        sg_u[:, lo:lo + FD], ALU.bitwise_or)
                last_or = kb.sig("vector", orw)
                dec_lane_or[l] = last_or
        kb.wait("vector", last_or, t_sc, tok_scaled_free)
        sc_last = None
        for g in range(ng):
            def fsc(e, asm0=asm0, g=g, sco=sco, scaled0=scaled0, scaled_t=scaled_t):
                return e.tensor_scalar(
                    scaled_t[:, scaled0 + g * 128: scaled0 + g * 128 + 128],
                    bigB[:, asm0 + g * 128: asm0 + g * 128 + 128],
                    scb[:, sco + g: sco + g + 1], None, ALU.mult)
            sc_last = kb.sig("vector", fsc)
        dec_asm_free[asm0] = sc_last
        return sc_last, last_or

    def dwn_gen():
        tok_pk = None
        tok_sc = None
        tok_nat = None
        G, FD, C, ng = 2, 352, 704, 44
        for ot in range(16):
            kb.wait("sync", tok_pk)
            t_pk = kb.sig("sync", lambda e, ot=ot: e.dma_start(
                out=pkbuf[:, 0:704],
                in_=dwn_pk[ot * 128:(ot + 1) * 128, :]), dma=True)
            kb.wait("gpsimd", tok_sc)
            t_sc = kb.sig("gpsimd", lambda e, ot=ot: e.dma_start(
                out=scb[:, 0:44],
                in_=dwn_sc[ot * 128:(ot + 1) * 128, :]), dma=True)
            yield
            units = [(j, hf * FD) for j in range(8) for hf in range(C // FD)]
            st_tok = [[None] * 8 for _ in range(G)]
            last_or = None
            for w0 in range(0, len(units), G):
                wave = units[w0:w0 + G]
                for l, (j, co) in enumerate(wave):
                    lo = l * FD
                    kb.wait("vector", dec_lane_or[l], dec_lane_st3[l], t_pk,
                            dec_asm_free[0])
                    st_tok[l][0] = kb.sig("vector", lambda e, j=j, co=co, lo=lo: e.tensor_scalar(
                        u_i[:, lo:lo + FD], pkbuf[:, co:co + FD],
                        4 * j, 7, ALU.logical_shift_right, ALU.bitwise_and))
                for l, (j, co) in enumerate(wave):
                    lo = l * FD
                    sbit = 4 * j + 3
                    msk = (1 << sbit) if sbit < 31 else -(1 << 31)
                    kb.wait("vector", dec_lane_st2[l])
                    if sbit <= 15:
                        st_tok[l][1] = kb.sig("vector", lambda e, co=co, lo=lo, msk=msk, sh=15 - sbit: e.tensor_scalar(
                            sg_i[:, lo:lo + FD], pkbuf[:, co:co + FD],
                            msk, sh, ALU.bitwise_and, ALU.logical_shift_left))
                    else:
                        st_tok[l][1] = kb.sig("vector", lambda e, co=co, lo=lo, msk=msk, sh=sbit - 15: e.tensor_scalar(
                            sg_i[:, lo:lo + FD], pkbuf[:, co:co + FD],
                            msk, sh, ALU.bitwise_and, ALU.logical_shift_right))
                for l in range(len(wave)):
                    lo = l * FD
                    kb.wait("vector", st_tok[l][1], dec_lane_or[l])
                    st_tok[l][2] = kb.sig("vector", lambda e, lo=lo: e.tensor_copy(
                        sg_u[:, lo:lo + FD], sg_i[:, lo:lo + FD]))
                    dec_lane_st2[l] = st_tok[l][2]
                for l in range(len(wave)):
                    lo = l * FD
                    kb.wait("vector", st_tok[l][0], dec_lane_st6[l])
                    st_tok[l][3] = kb.sig("vector", lambda e, lo=lo: e.tensor_copy(
                        f_b[:, lo:lo + FD], u_i[:, lo:lo + FD]))
                    dec_lane_st3[l] = st_tok[l][3]
                for l in range(len(wave)):
                    lo = l * FD
                    kb.wait("vector", st_tok[l][3])
                    st_tok[l][4] = kb.sig("vector", lambda e, lo=lo: e.tensor_scalar(
                        d1_b[:, lo:lo + FD], f_b[:, lo:lo + FD], 2.0, 8.0,
                        ALU.mult, ALU.subtract))
                for l in range(len(wave)):
                    lo = l * FD
                    kb.wait("vector", st_tok[l][4])
                    st_tok[l][5] = kb.sig("vector", lambda e, lo=lo: e.scalar_tensor_tensor(
                        d2_b[:, lo:lo + FD], f_b[:, lo:lo + FD], 2.0,
                        d1_b[:, lo:lo + FD], ALU.subtract, ALU.max))
                for l in range(len(wave)):
                    lo = l * FD
                    kb.wait("vector", st_tok[l][5])
                    st_tok[l][6] = kb.sig("vector", lambda e, lo=lo: e.scalar_tensor_tensor(
                        d1_b[:, lo:lo + FD], f_b[:, lo:lo + FD], 0.5,
                        d2_b[:, lo:lo + FD], ALU.mult, ALU.max))
                    dec_lane_st6[l] = st_tok[l][6]
                for l, (j, co) in enumerate(wave):
                    lo = l * FD
                    kb.wait("vector", st_tok[l][6], st_tok[l][2])

                    def orw(e, co=co, j=j, lo=lo):
                        outap = bass.AP(bigB.ap().tensor, 8 * co + j,
                                        [[14464, 128], [8, FD]]).bitcast(U16)
                        return e.tensor_tensor(
                            outap, d1_b[:, lo:lo + FD].bitcast(U16),
                            sg_u[:, lo:lo + FD], ALU.bitwise_or)
                    last_or = kb.sig("vector", orw)
                    dec_lane_or[l] = last_or
                yield
            tok_pk = last_or
            kb.wait("vector", last_or, t_sc, tok_nat)
            sc_last = None
            for g in range(ng):
                def fsc(e, g=g):
                    return e.tensor_scalar(
                        bigB[:, 5632 + g * 128: 5632 + g * 128 + 128],
                        bigB[:, g * 128: g * 128 + 128],
                        scb[:, g: g + 1], None, ALU.mult)
                sc_last = kb.sig("vector", fsc)
            dec_asm_free[0] = sc_last
            tok_sc = sc_last
            yield
            kb.wait("sync", sc_last)
            tok_nat = kb.sig("sync", lambda e, ot=ot: e.dma_start(
                out=bass.AP(wdn_nat.tensor, ot * 128 * 5632,
                            [[5632, 128], [1, 5632]]),
                in_=bigB[:, 5632:11264]), dma=True)
            yield

    def phase2a(tok_idn):
        tok_ld = [None, None]
        tok_stage = [None, None]
        tok_psT = None
        tok_rd = [None, None]
        t_sp = None
        for ot in range(16):
            slot = ot % 2
            kb.wait("sync", tok_rd[slot])
            t_ld = kb.sig("sync", lambda e, ot=ot, slot=slot: e.dma_start(
                out=bigA[:, slot * 5632: slot * 5632 + 5632],
                in_=bass.AP(wdn_nat.tensor, ot * 128 * 5632,
                            [[5632, 128], [1, 5632]])), dma=True)
            t_tp = None
            for g0 in range(0, 44, 16):
                gc = min(16, 44 - g0)
                sl2 = (g0 // 16) % 2
                kb.wait("tensor", t_ld, tok_psT, tok_idn)
                for gg in range(gc):
                    def ftp(e, g0=g0, gg=gg, slot=slot):
                        dst = pB[:, gg * 64: gg * 64 + 64].bitcast(BF16)
                        return e.transpose(
                            dst,
                            bigA[:, slot * 5632 + (g0 + gg) * 128: slot * 5632 + (g0 + gg) * 128 + 128],
                            idn[:])
                    if gg == gc - 1:
                        t_tp = kb.sig("tensor", ftp)
                    else:
                        kb.op("tensor", ftp)
                kb.wait("scalar", t_tp, tok_stage[sl2])
                t_ev = kb.sig("scalar", lambda e, gc=gc, sl2=sl2: e.copy(
                    bigA[:, 16384 + sl2 * 2048: 16384 + sl2 * 2048 + gc * 128],
                    pB[:, 0:gc * 64].bitcast(BF16)))
                tok_psT = t_ev
                kb.wait("sync", t_ev)

                def fsp(e, ot=ot, g0=g0, gc=gc, sl2=sl2):
                    dst = bass.AP(wdn_sp.tensor, (ot * 44 + g0) * 16384,
                                  [[128, 128], [16384, gc], [1, 128]])
                    srcp = bass.AP(bigA.ap().tensor, 16384 + sl2 * 2048,
                                   [[32768, 128], [128, gc], [1, 128]])
                    return e.dma_start(out=dst, in_=srcp)
                t_sp = kb.sig("sync", fsp, dma=True)
                tok_stage[sl2] = t_sp
            tok_rd[slot] = t_tp
        return t_sp

    def phase2b():
        LH = [16384, 20480]
        tok_pkslot = [None, None]
        tok_scslot2 = [None, None]
        tok_scaled_g = None
        tok_scaled_u = None
        tok_lh = [None, None]
        tok_evA = None
        tok_pA = None
        tok_pB = None
        tok_g = None
        tok_h = [None, None]
        ev_toks = [None, None]

        def load_pair(it):
            slot = it % 2
            kb.wait("sync", tok_pkslot[slot])
            t1 = kb.sig("sync", lambda e, it=it, slot=slot: e.dma_start(
                out=pkbuf[:, slot * 512: slot * 512 + 256],
                in_=gup_pk[it * 128:(it + 1) * 128, :]), dma=True)
            t2 = kb.sig("sync", lambda e, it=it, slot=slot: e.dma_start(
                out=pkbuf[:, slot * 512 + 256: slot * 512 + 512],
                in_=gup_pk[(it + 44) * 128:(it + 45) * 128, :]), dma=True)
            kb.wait("gpsimd", tok_scslot2[slot])
            t3 = kb.sig("gpsimd", lambda e, it=it, slot=slot: e.dma_start(
                out=scb[:, slot * 16: slot * 16 + 16],
                in_=gup_sc[it * 128:(it + 1) * 128, :]), dma=True)
            t4 = kb.sig("gpsimd", lambda e, it=it, slot=slot: e.dma_start(
                out=scb[:, 32 + slot * 16: 32 + slot * 16 + 16],
                in_=gup_sc[(it + 44) * 128:(it + 45) * 128, :]), dma=True)
            return t1, t2, t3, t4

        def dec_pair(it, toks):
            slot = it % 2
            nonlocal tok_scaled_g, tok_scaled_u
            t1, t2, t3, t4 = toks
            sg_tok, or_g = emit_decode(t1, slot * 512, 256, 256, 3, 16,
                                       slot * 16, 0, bigB, 4096,
                                       tok_scaled_g, t3)
            su_tok, or_u = emit_decode(t2, slot * 512 + 256, 256, 256, 3, 16,
                                       32 + slot * 16, 2048, bigB, 6144,
                                       tok_scaled_u, t4)
            tok_scslot2[slot] = su_tok
            return sg_tok, su_tok, or_u

        def tp_pair(it, sg_tok, su_tok):
            nonlocal tok_evA, tok_scaled_g, tok_scaled_u
            ls = it % 2
            ts_g = None
            ts_u = None
            kb.wait("tensor", sg_tok, tok_g, tok_lh[ls])
            for gg in range(16):
                def ftp(e, gg=gg):
                    return e.transpose(pA[:, gg * 64: gg * 64 + 64].bitcast(BF16),
                                       bigB[:, 4096 + gg * 128: 4096 + gg * 128 + 128],
                                       idn[:])
                if gg == 15:
                    ts_g = kb.sig("tensor", ftp)
                else:
                    kb.op("tensor", ftp)
            tok_scaled_g = ts_g
            kb.wait("tensor", su_tok)
            for gg in range(16):
                def ftp(e, gg=gg):
                    return e.transpose(pA[:, 1024 + gg * 64: 1024 + gg * 64 + 64].bitcast(BF16),
                                       bigB[:, 6144 + gg * 128: 6144 + gg * 128 + 128],
                                       idn[:])
                if gg == 15:
                    ts_u = kb.sig("tensor", ftp)
                else:
                    kb.op("tensor", ftp)
            tok_scaled_u = ts_u
            kb.wait("scalar", ts_g)
            e1 = kb.sig("scalar", lambda e, ls=ls: e.copy(
                bigA[:, LH[ls]:LH[ls] + 2048], pA[:, 0:1024].bitcast(BF16)))
            kb.wait("scalar", ts_u)
            e2 = kb.sig("scalar", lambda e, ls=ls: e.copy(
                bigA[:, LH[ls] + 2048:LH[ls] + 4096],
                pA[:, 1024:2048].bitcast(BF16)))
            tok_evA = e2
            return e1, e2

        toks = load_pair(0)
        sg_tok, su_tok, or_u = dec_pair(0, toks)
        tok_pkslot[0] = or_u
        ev_toks = tp_pair(0, sg_tok, su_tok)

        for it in range(NI0):
            ls = it % 2
            my_ev = ev_toks
            if it + 1 < NI0:
                toks = load_pair(it + 1)
            t_gm = mm_group(pA, lambda ki, ls=ls: bigA[:, LH[ls] + ki * 128: LH[ls] + ki * 128 + 128],
                            lambda ki, n0, n1: xt[:, ki * S + n0: ki * S + n1],
                            NT, S, pe_wait=(my_ev[0], my_ev[1], tok_pA))
            t_um = mm_group(pB, lambda ki, ls=ls: bigA[:, LH[ls] + 2048 + ki * 128: LH[ls] + 2048 + ki * 128 + 128],
                            lambda ki, n0, n1: xt[:, ki * S + n0: ki * S + n1],
                            NT, S, pe_wait=(tok_pB,))
            tok_lh[ls] = t_um
            kb.wait("scalar", t_gm, tok_g)
            t_si = kb.sig("scalar", lambda e: e.activation(
                bigB[:, 8192:10240], pA, ACTF.Silu))
            tok_pA = t_si
            tok_g = t_si
            if it + 1 < NI0:
                sg_tok, su_tok, or_u = dec_pair(it + 1, toks)
                tok_pkslot[(it + 1) % 2] = or_u
                ev_toks = tp_pair(it + 1, sg_tok, su_tok)
            hf = fA if it % 2 == 0 else fB
            kb.wait("vector", t_si, t_um, tok_h[it % 2])
            t_hm = kb.sig("vector", lambda e, hf=hf: e.tensor_tensor(
                hf[:], pB, bigB[:, 8192:10240], ALU.mult))
            tok_pB = t_hm
            tok_g = t_hm
            kb.wait("gpsimd", t_hm)
            t_hs = kb.sig("gpsimd", lambda e, it=it, hf=hf: e.dma_start(
                out=bass.AP(h_sp.tensor, it * 128 * S, [[S, 128], [1, S]]),
                in_=hf[:]), dma=True)
            tok_h[it % 2] = t_hs

        kb.barrier()

        for ch in range(4):
            c0 = ch * 512
            t_hc = kb.sig("sync", lambda e, c0=c0: e.dma_start(
                out=bass.AP(bigA.ap().tensor, 0,
                            [[32768, 128], [512, 44], [1, 512]]),
                in_=bass.AP(h_sp.tensor, c0,
                            [[S, 128], [128 * S, 44], [1, 512]])), dma=True)
            tok_wsl = [None, None]
            tok_ybank = [None, None, None, None]
            ss_tok = None
            for dm in range(16):
                bank = dm % 4
                mmlast = None
                for gs in range(4):
                    sl = (dm * 4 + gs) % 2
                    kb.wait("sync", tok_wsl[sl])
                    t_l = kb.sig("sync", lambda e, dm=dm, gs=gs, sl=sl: e.dma_start(
                        out=bass.AP(wb.ap().tensor, sl * 2048,
                                    [[4096, 128], [128, 11], [1, 128]]),
                        in_=bass.AP(wdn_sp.tensor, (dm * 44 + gs * 11) * 16384,
                                    [[128, 128], [16384, 11], [1, 128]])), dma=True)
                    kb.wait("tensor", t_l, t_hc,
                            tok_ybank[bank] if gs == 0 else None)
                    for ki in range(11):
                        st = (gs == 0 and ki == 0)
                        sp = (gs == 3 and ki == 10)

                        def f(e, ki=ki, gs=gs, sl=sl, bank=bank, st=st, sp=sp):
                            return e.matmul(
                                pA[:, bank * 512: bank * 512 + 512],
                                wb[:, sl * 2048 + ki * 128: sl * 2048 + ki * 128 + 128],
                                bass.AP(bigA.ap().tensor, (gs * 11 + ki) * 512,
                                        [[32768, 128], [1, 512]]),
                                start=st, stop=sp)
                        if ki == 10:
                            mmlast = kb.sig("tensor", f)
                        else:
                            kb.op("tensor", f)
                    tok_wsl[sl] = mmlast
                kb.wait("scalar", mmlast)
                t_ybf = kb.sig("scalar", lambda e, dm=dm, bank=bank: e.copy(
                    bigB[:, 2048 + dm * 512: 2048 + dm * 512 + 512],
                    pA[:, bank * 512: bank * 512 + 512]))
                kb.wait("vector", t_ybf)
                sq = fA if dm % 2 == 0 else fB
                t_sq = kb.sig("vector", lambda e, dm=dm, sq=sq: e.tensor_tensor(
                    sq[:, 0:512],
                    bigB[:, 2048 + dm * 512: 2048 + dm * 512 + 512],
                    bigB[:, 2048 + dm * 512: 2048 + dm * 512 + 512], ALU.mult))
                kb.wait("tensor", t_sq)

                def fss(e, dm=dm, sq=sq):
                    return e.matmul(pB[0:1, 0:512], onescol[:], sq[:, 0:512],
                                    start=(dm == 0), stop=(dm == 15))
                if dm == 15:
                    ss_tok = kb.sig("tensor", fss)
                else:
                    kb.op("tensor", fss)
                tok_ybank[bank] = t_ybf
            kb.wait("vector", ss_tok)
            t_r1 = kb.sig("vector", lambda e: e.tensor_scalar(
                rows[0:1, 0:512], pB[0:1, 0:512], 1.0 / 2048.0, None, ALU.mult))
            kb.wait("scalar", t_r1)
            t_r2 = kb.sig("scalar", lambda e: e.activation(
                bigb_row(10240, 512), rows[0:1, 0:512], ACTF.Ln,
                bias=epsb[0:1, 1:2]))
            kb.wait("scalar", t_r2)
            t_r3 = kb.sig("scalar", lambda e: e.activation(
                bigb_row(11264, 512), bigb_row(10240, 512), ACTF.Exp, scale=-0.5))
            kb.wait("tensor", t_r3, t_r1)
            t_rb = kb.sig("tensor", lambda e: e.matmul(
                pB[:, 512:1024], onesrow[0:1, :], bigb_row(11264, 512),
                start=True, stop=True))
            utoks = []
            for dm in range(16):
                tb = fA if dm % 2 == 0 else fB
                kb.wait("vector", t_rb)
                t1 = kb.sig("vector", lambda e, dm=dm, tb=tb: e.tensor_tensor(
                    tb[:, 0:512],
                    bigB[:, 2048 + dm * 512: 2048 + dm * 512 + 512],
                    pB[:, 512:1024], ALU.mult))
                kb.wait("vector", t1)
                t2 = kb.sig("vector", lambda e, dm=dm, tb=tb, c0=c0: e.tensor_tensor(
                    bigB[:, 2048 + dm * 512: 2048 + dm * 512 + 512],
                    tb[:, 0:512], xt[:, dm * S + c0: dm * S + c0 + 512],
                    ALU.add))
                utoks.append(t2)
            tok_pe = emit_stats(
                lambda t: bigB[:, 2048 + t * 512: 2048 + t * 512 + 512], 512,
                pA[0:1, 0:512], pA[0:1, 512:1024], utoks=utoks)
            tok_m, tok_rb2 = emit_mr_rows_bc(
                tok_pe, pA[0:1, 0:512], pA[0:1, 512:1024], 512, 0, 2048,
                pA[:, 1024:1536], pA[:, 1536:2048])
            for t in range(NT):
                tb = fA if t % 2 == 0 else fB
                kb.wait("vector", tok_m, tok_rb2)
                t_s = kb.sig("vector", lambda e, t=t, tb=tb: e.tensor_tensor(
                    tb[:, 0:512],
                    bigB[:, 2048 + t * 512: 2048 + t * 512 + 512],
                    pA[:, 1024:1536], ALU.subtract))
                kb.wait("vector", t_s)
                kb.sig("vector", lambda e, t=t, tb=tb, c0=c0: e.tensor_tensor(
                    xt[:, t * S + c0: t * S + c0 + 512], tb[:, 0:512],
                    pA[:, 1536:2048], ALU.mult))
            kb.barrier()

    def phase3():
        for h in range(16):
            t_wq = wcol_load(wq1, h * 128, 0)
            t_q = mm_group(pA, lambda ki: wb[:, ki * 128: ki * 128 + 128],
                           lambda ki, n0, n1: xt[:, ki * S + n0: ki * S + n1],
                           NT, S, pe_wait=(t_wq,))
            kb.wait("scalar", t_q)
            t_qe = kb.sig("scalar", lambda e: e.copy(bigB[:, 0:2048], pA))
            kb.wait("gpsimd", t_q)
            t_wk = wcol_load(wk1, h * 128, 1)
            t_kmm = mm_group(pB, lambda ki: wb[:, 2048 + ki * 128: 2048 + ki * 128 + 128],
                             lambda ki, n0, n1: xt[:, ki * S + n0: ki * S + n1],
                             NT, S, pe_wait=(t_wk,))
            kb.wait("scalar", t_kmm)
            t_ke = kb.sig("scalar", lambda e: e.copy(bigB[:, 2048:4096], pB))
            kb.wait("gpsimd", t_qe)
            t_wv = wcol_load(wv1, h * 128, 0)
            kb.wait("tensor", t_wv, t_qe, t_ke)
            t_vmm = None
            for st_ in range(16):
                for dt in range(NT):
                    def fv(e, st_=st_, dt=dt):
                        return e.matmul(
                            pA[:, st_ * 128: st_ * 128 + 128],
                            xt[:, dt * S + st_ * 128: dt * S + st_ * 128 + 128],
                            wb[:, dt * 128: dt * 128 + 128],
                            start=(dt == 0), stop=(dt == NT - 1))
                    if st_ == 15 and dt == NT - 1:
                        t_vmm = kb.sig("tensor", fv)
                    else:
                        kb.op("tensor", fv)
            kb.wait("scalar", t_vmm)

            def fve(e):
                dst = bass.AP(bigB.ap().tensor, 12288,
                              [[14464, 128], [129, 16], [1, 128]])
                srcp = bass.AP(pAh, 0, [[2048, 128], [128, 16], [1, 128]])
                return e.copy(dst, srcp)
            t_ve = kb.sig("scalar", fve)

            tok_sbank = [t_ve, t_ve, t_ve, t_ve]
            tok_avbank = [t_ve] * 4
            tok_avdone = None
            t_p1_prev = None
            for qc in range(4):
                p_toks = {}
                for kt in range(4 * qc + 4):
                    bank = kt % 4
                    kb.wait("tensor", t_ke, t_ve, tok_sbank[bank])

                    def fs(e, kt=kt, qc=qc, bank=bank):
                        return e.matmul(
                            pA[:, bank * 512: bank * 512 + 512],
                            bigB[:, 2048 + kt * 128: 2048 + kt * 128 + 128],
                            bigB[:, qc * 512: qc * 512 + 512],
                            start=True, stop=True)
                    t_smm = kb.sig("tensor", fs)
                    diag = (kt >= 4 * qc)
                    if not diag:
                        kb.wait("scalar", t_smm, tok_avdone)
                        t_p = kb.sig("scalar", lambda e, kt=kt, bank=bank: e.activation(
                            bigB[:, 4096 + kt * 512: 4096 + kt * 512 + 512],
                            pA[:, bank * 512: bank * 512 + 512],
                            ACTF.Exp, scale=SC1))
                        tok_sbank[bank] = t_p
                        p_toks[kt] = t_p
                    else:
                        off = (kt - 4 * qc) * 128
                        kb.wait("scalar", t_smm, t_p1_prev)
                        t_p1 = kb.sig("scalar", lambda e, bank=bank: e.activation(
                            fA[:, 0:512],
                            pA[:, bank * 512: bank * 512 + 512],
                            ACTF.Exp, scale=SC1))
                        tok_sbank[bank] = t_p1
                        kb.wait("vector", t_p1, tok_avdone)
                        t_p2 = kb.sig("vector", lambda e, kt=kt, off=off: e.tensor_tensor(
                            bigB[:, 4096 + kt * 512: 4096 + kt * 512 + 512],
                            fA[:, 0:512],
                            maskw[:, 512 - off: 1024 - off], ALU.mult))
                        p_toks[kt] = t_p2
                        t_p1_prev = t_p2
                for qs in range(4):
                    nkt = 4 * qc + qs + 1
                    bank = qs % 4
                    kb.wait("tensor", tok_avbank[bank],
                            *[p_toks[k] for k in range(nkt)])
                    t_av = None
                    for kt in range(nkt):
                        def fav(e, kt=kt, qc=qc, qs=qs, bank=bank, nkt=nkt):
                            return e.matmul(
                                pB[:, bank * 512: bank * 512 + 129],
                                bigB[:, 4096 + kt * 512 + qs * 128: 4096 + kt * 512 + qs * 128 + 128],
                                bigB[:, 12288 + kt * 129: 12288 + kt * 129 + 129],
                                start=(kt == 0), stop=(kt == nkt - 1))
                        if kt == nkt - 1:
                            t_av = kb.sig("tensor", fav)
                        else:
                            kb.op("tensor", fav)
                    kb.wait("vector", t_av)
                    t_rc = kb.sig("vector", lambda e, bank=bank: e.reciprocal(
                        recb[:, 0:1], pB[:, bank * 512 + 128: bank * 512 + 129]))
                    kb.wait("vector", t_rc)
                    t_an = kb.sig("vector", lambda e, bank=bank, qs=qs: e.tensor_scalar(
                        fB[:, qs * 128: qs * 128 + 128],
                        pB[:, bank * 512: bank * 512 + 128],
                        recb[:, 0:1], None, ALU.mult))
                    kb.wait("tensor", t_an)
                    t_tp = kb.sig("tensor", lambda e, bank=bank, qs=qs: e.transpose(
                        pB[:, bank * 512: bank * 512 + 64].bitcast(BF16),
                        fB[:, qs * 128: qs * 128 + 128], idn[:]))
                    kb.wait("scalar", t_tp)
                    t_ae = kb.sig("scalar", lambda e, h=h, qc=qc, qs=qs, bank=bank: e.copy(
                        bigA[:, h * S + qc * 512 + qs * 128: h * S + qc * 512 + qs * 128 + 128],
                        pB[:, bank * 512: bank * 512 + 64].bitcast(BF16)))
                    tok_avbank[bank] = t_ae
                    tok_avdone = t_av
            kb.barrier()

        utoks = []
        slot_tok = [None, None]
        ps_tok = [None, None]
        for ft in range(NT):
            sl = ft % 2
            kb.wait("gpsimd", slot_tok[sl])
            t_wo = wcol_load(wo1, ft * 128, sl)
            ps = pA if sl == 0 else pB
            t_mm = mm_group(ps,
                            lambda ki, sl=sl: wb[:, sl * 2048 + ki * 128: sl * 2048 + ki * 128 + 128],
                            lambda ki, n0, n1: bigA[:, ki * S + n0: ki * S + n1],
                            NT, S, pe_wait=(t_wo, ps_tok[sl]))
            slot_tok[sl] = t_mm
            tb = fA if sl == 0 else fB
            kb.wait("vector", t_mm)
            t_u1 = kb.sig("vector", lambda e, ps=ps, ft=ft, tb=tb: e.scalar_tensor_tensor(
                tb[:], ps, bo[:, ft:ft + 1], xt[:, ft * S:(ft + 1) * S],
                ALU.add, ALU.add))
            ps_tok[sl] = t_u1
            kb.wait("vector", t_u1)
            t_u2 = kb.sig("vector", lambda e, ft=ft, tb=tb: e.tensor_copy(
                xt[:, ft * S:(ft + 1) * S], tb[:]))
            utoks.append(t_u2)
        return utoks

    def phase4(stop=None):
        tok_pe = emit_stats(lambda t: xt[:, t * S:(t + 1) * S], 2048,
                            pA[0:1, :], pB[0:1, :])
        tok_m, tok_rb = emit_mr_rows_bc(tok_pe, pA[0:1, :], pB[0:1, :],
                                        2048, 2, 2048, pA, pB)
        for t in range(NT):
            tb = fA if t % 2 == 0 else fB
            kb.wait("vector", tok_m, tok_rb)
            t_s = kb.sig("vector", lambda e, t=t, tb=tb: e.tensor_tensor(
                tb[:], xt[:, t * S:(t + 1) * S], pA, ALU.subtract))
            kb.wait("vector", t_s)
            kb.sig("vector", lambda e, t=t, tb=tb: e.tensor_tensor(
                bigA[:, t * S:(t + 1) * S], tb[:], pB, ALU.mult))
        kb.barrier()
        if stop == "p4a":
            return
        t_am = kb.sig("vector", lambda e: e.tensor_scalar(
            bigB[:, 0:2048].bitcast(U16), bigA[:, 0:S].bitcast(U16),
            0x7FFF, None, ALU.bitwise_and))
        acc = 0
        for t in range(1, NT):
            tmp = fA if t % 2 == 0 else fB
            kb.wait("vector", t_am)
            t_a1 = kb.sig("vector", lambda e, t=t, tmp=tmp: e.tensor_scalar(
                tmp.ap().bitcast(U16), bigA[:, t * S:(t + 1) * S].bitcast(U16),
                0x7FFF, None, ALU.bitwise_and))
            kb.wait("vector", t_a1)
            oth = 1 - acc
            t_am = kb.sig("vector", lambda e, acc=acc, oth=oth, tmp=tmp: e.tensor_tensor(
                bigB[:, oth * 2048: oth * 2048 + 2048],
                bigB[:, acc * 2048: acc * 2048 + 2048], tmp[:], ALU.max))
            acc = oth
        curoff = acc * 2048
        if stop == "p4b1":
            return
        kb.wait("tensor", t_am, tok_idn)
        tp_toks = []
        for b in range(16):
            def ftp(e, b=b, curoff=curoff):
                return e.transpose(pA[:, b * 64: b * 64 + 64].bitcast(BF16),
                                   bigB[:, curoff + b * 128: curoff + b * 128 + 128],
                                   idn[:])
            tp_toks.append(kb.sig("tensor", ftp))
        rd_toks = []
        kb.wait("vector", tp_toks[-1])
        for b in range(16):
            rd_toks.append(kb.sig("vector", lambda e, b=b: e.tensor_reduce(
                cmax[:, b:b + 1], pA[:, b * 64: b * 64 + 64].bitcast(BF16),
                AX.X, ALU.max)))
        if stop == "p4b2":
            return
        kb.wait("vector", *rd_toks)
        t1 = kb.sig("vector", lambda e: e.tensor_scalar(
            rsqt[:], cmax[:], 1.0 / 127.0, None, ALU.mult))
        kb.wait("vector", t1)
        t2 = kb.sig("vector", lambda e: e.reciprocal(sqt[:], rsqt[:]))
        row_dm = []
        t_evp = None
        rsq_row = bigb_row(0, 2048)
        t_d1p = None
        for (src, row_i, tk) in ((sqt, 0, t2), (rsqt, 1, t1)):
            kb.wait("tensor", tk, t_evp)
            kb.wait("tensor", tok_idnf)
            t_tp = kb.sig("tensor", lambda e, src=src: e.transpose(
                pBh[0:16, 0:128], src[:], idnf[:]))
            kb.wait("scalar", t_tp, t_d1p)
            t_ev = kb.sig("scalar", lambda e: e.copy(tmp16[:], pBh[0:16, 0:128]))
            t_evp = t_ev
            kb.wait("gpsimd", t_ev)
            t_d1 = kb.sig("gpsimd", lambda e, row_i=row_i: e.dma_start(
                out=bass.AP(row_sp.tensor, row_i * 2048,
                            [[128, 16], [1, 128]]),
                in_=tmp16[:]), dma=True)
            t_d1p = t_d1
            kb.wait("gpsimd", t_d1)
            t_dm = kb.sig("gpsimd", lambda e, row_i=row_i: e.dma_start(
                out=(rows[0:1, :] if row_i == 0 else rsq_row),
                in_=bass.AP(row_sp.tensor, row_i * 2048,
                            [[2048, 1], [1, 2048]])), dma=True)
            row_dm.append(t_dm)
        if stop == "p4b3":
            return
        tok_sq = emit_bcast(rows[0:1, :], pA, 2048, row_dm[0], row_dm[1])
        tok_rsq = emit_bcast(rsq_row, pB, 2048, row_dm[0], row_dm[1])
        kb.wait("scalar", tok_rsq)
        t_rsb = kb.sig("scalar", lambda e: e.copy(rsqb_bf, pB))
        if stop == "p4b4":
            return
        def qfl(nq):
            return bass.AP(bigB.ap().tensor, 4096 + nq * 1024,
                           [[14464, 128], [1, 1024]]).bitcast(F32)

        def uil(nq):
            return bass.AP(bigB.ap().tensor, 8192 + nq * 1024,
                           [[14464, 128], [1, 1024]]).bitcast(I32)

        qtoks = []
        tb_p = [None] * 4
        tc_p = [None] * 4
        td_p = [None] * 4
        for t in range(NT):
            tas = [None] * 4
            for nq in range(4):
                kb.wait("vector", tok_sq, t_rsb, tb_p[nq])
                tas[nq] = kb.sig("vector", lambda e, t=t, nq=nq: e.tensor_tensor(
                    pB[:, nq * 512: nq * 512 + 512],
                    bigA[:, t * S + nq * 512: t * S + nq * 512 + 512],
                    pA[:, nq * 512: nq * 512 + 512], ALU.mult))
            tbs = [None] * 4
            for nq in range(4):
                kb.wait("vector", tas[nq], tc_p[nq])
                tbs[nq] = kb.sig("vector", lambda e, nq=nq: e.tensor_scalar(
                    qfl(nq), pB[:, nq * 512: nq * 512 + 512],
                    -128.0, 127.0, ALU.max, ALU.min))
            tcs = [None] * 4
            for nq in range(4):
                kb.wait("vector", tbs[nq], td_p[nq])
                tcs[nq] = kb.sig("vector", lambda e, nq=nq: e.tensor_copy(
                    uil(nq), qfl(nq)))
            tds = [None] * 4
            for nq in range(4):
                kb.wait("vector", tcs[nq])
                tds[nq] = kb.sig("vector", lambda e, t=t, nq=nq: e.tensor_copy(
                    bigA[:, t * S + nq * 512: t * S + nq * 512 + 512],
                    uil(nq)))
                qtoks.append(tds[nq])
            tb_p = tbs
            tc_p = tcs
            td_p = tds
        if stop == "p4b":
            return
        for t in range(NT):
            for cq in range(4):
                kb.sig("gpsimd", lambda e, t=t, cq=cq: e.dma_start(
                    out=bass.AP(x_sp.tensor, cq * (128 * 16 * 512) + t * 512,
                                [[16 * 512, 128], [1, 512]]),
                    in_=xt[:, t * S + cq * 512: t * S + cq * 512 + 512]),
                    dma=True)
        kb.barrier()

        GS = [0, 10240]
        US = [2048, 12288]
        t_ld_g = [None, None]
        t_ld_u = [None, None]
        tok_st_g = [None, None]
        tok_st_u = [None, None]
        tok_mm_g = [None, None]
        tok_mm_u = [None, None]
        t_sg2 = [None, None]
        t_su2 = [None, None]
        tok_pA = None
        tok_pB = None
        tok_silu = None
        tok_X = [[], []]
        t_p_prev = None
        tok_wdsg = None
        tok_wdsp = None

        def ld_w(ht):
            par = ht % 2
            kb.wait("gpsimd", tok_st_g[par])
            t_ld_g[par] = kb.sig("gpsimd", lambda e, ht=ht, par=par: e.dma_start(
                out=xt[:, par * 2048: par * 2048 + 2048],
                in_=bass.AP(wg1t.tensor, ht * 262144,
                            [[2048, 128], [1, 2048]])), dma=True)
            kb.wait("gpsimd", tok_st_u[par])
            t_ld_u[par] = kb.sig("gpsimd", lambda e, ht=ht, par=par: e.dma_start(
                out=xt[:, 4096 + par * 2048: 4096 + par * 2048 + 2048],
                in_=bass.AP(wu1t.tensor, ht * 262144,
                            [[2048, 128], [1, 2048]])), dma=True)

        def sign_w(ht):
            par = ht % 2
            kb.wait("scalar", t_ld_g[par], tok_mm_g[par])
            t_sg2[par] = kb.sig("scalar", lambda e, par=par: e.sign(
                bigB[:, GS[par]:GS[par] + 2048],
                xt[:, par * 2048: par * 2048 + 2048]))
            tok_st_g[par] = t_sg2[par]
            kb.wait("scalar", t_ld_u[par], tok_mm_u[par])
            t_su2[par] = kb.sig("scalar", lambda e, par=par: e.sign(
                bigB[:, US[par]:US[par] + 2048],
                xt[:, 4096 + par * 2048: 4096 + par * 2048 + 2048]))
            tok_st_u[par] = t_su2[par]

        ld_w(0)
        ld_w(1)
        sign_w(0)
        for ht in range(NH1):
            par = ht % 2
            X = fA if ht % 2 == 0 else fB
            Y = fB if ht % 2 == 0 else fA
            if ht + 2 < NH1:
                ld_w(ht + 2)
            if ht + 1 < NH1:
                sign_w(ht + 1)
            t_gm = mm_group(pA, lambda ki, par=par: bigB[:, GS[par] + ki * 128: GS[par] + ki * 128 + 128],
                            lambda ki, n0, n1: bigA[:, ki * S + n0: ki * S + n1],
                            NT, S, pe_wait=(t_sg2[par], tok_pA))
            tok_mm_g[par] = t_gm
            t_um = mm_group(pB, lambda ki, par=par: bigB[:, US[par] + ki * 128: US[par] + ki * 128 + 128],
                            lambda ki, n0, n1: bigA[:, ki * S + n0: ki * S + n1],
                            NT, S, pe_wait=(t_su2[par], tok_pB))
            tok_mm_u[par] = t_um
            kb.wait("vector", t_gm, t_p_prev, *tok_X[ht % 2])
            t_g1 = kb.sig("vector", lambda e, X=X: e.tensor_tensor(
                X[:], pA, rsqb_bf, ALU.mult))
            tok_pA = t_g1
            kb.wait("scalar", t_g1, tok_silu)
            t_g2 = kb.sig("scalar", lambda e, X=X: e.activation(
                bigB[:, 4096:6144], X[:], ACTF.Silu))
            kb.wait("vector", t_um, *tok_X[(ht + 1) % 2])
            t_u1 = kb.sig("vector", lambda e, Y=Y: e.tensor_tensor(
                Y[:], pB, rsqb_bf, ALU.mult))
            tok_pB = t_u1
            kb.wait("vector", t_g2, t_u1)
            t_p = kb.sig("vector", lambda e, X=X, Y=Y: e.tensor_tensor(
                X[:], bigB[:, 4096:6144], Y[:], ALU.mult))
            tok_silu = t_p
            t_p_prev = t_p
            kb.wait("scalar", t_p)
            t_ps_l = []
            for cq in range(4):
                t_ps_l.append(kb.sig("scalar", lambda e, ht=ht, X=X, cq=cq: e.dma_start(
                    out=bass.AP(p_sp.tensor, cq * (128 * 64 * 512) + ht * 512,
                                [[64 * 512, 128], [1, 512]]),
                    in_=X[:, cq * 512: cq * 512 + 512]), dma=True))
            tok_X[ht % 2] = t_ps_l
            kb.wait("gpsimd", tok_wdsg)
            t_w = kb.sig("gpsimd", lambda e, ht=ht: e.dma_start(
                out=bigB[:, 6144:8192],
                in_=bass.AP(wd1t.tensor, ht * 128 * 2048,
                            [[2048, 128], [1, 2048]])), dma=True)
            kb.wait("scalar", t_w, tok_wdsp)
            t_wsg = kb.sig("scalar", lambda e: e.sign(
                bigB[:, 8192:10240], bigB[:, 6144:8192]))
            tok_wdsg = t_wsg
            kb.wait("sync", t_wsg)

            def fwdsp(e, ht=ht):
                dst = bass.AP(wds_sp.tensor,
                              (ht // 16) * 128 * 2048 + (ht % 16) * 128,
                              [[2048, 128], [4 * 128 * 2048, 16], [1, 128]])
                srcp = bass.AP(bigB.ap().tensor, 8192,
                               [[14464, 128], [128, 16], [1, 128]])
                return e.dma_start(out=dst, in_=srcp)
            tok_wdsp = kb.sig("sync", fwdsp, dma=True)
        kb.barrier()
        if stop == "p4c":
            return
        kb.barrier()
        if stop == "p4d":
            return
        pc_toks = [None] * 4
        xr_toks = [None] * 4

        def ld_pc(ch, *wait):
            kb.wait("sync", *wait)
            pc_toks[ch] = kb.sig("sync", lambda e, ch=ch: e.dma_start(
                out=xt[:, 0:32768],
                in_=bass.AP(p_sp.tensor, ch * (128 * 64 * 512),
                            [[64 * 512, 128], [1, 64 * 512]])), dma=True)

        def ld_xr(ch, *wait):
            kb.wait("gpsimd", *wait)
            xr_toks[ch] = kb.sig("gpsimd", lambda e, ch=ch: e.dma_start(
                out=bass.AP(bigA.ap().tensor, 8192,
                            [[32768, 128], [1, 8192]]),
                in_=bass.AP(x_sp.tensor, ch * (128 * 16 * 512),
                            [[16 * 512, 128], [1, 16 * 512]])), dma=True)

        ld_pc(0)
        ld_xr(0)
        for ch in range(4):
            c0 = ch * 512
            t_pc = pc_toks[ch]
            t_xr = xr_toks[ch]
            tok_wsl = [None, None]
            tok_yb = [None] * 4
            for dm in range(16):
                bank = dm % 4
                mml = None
                for wv in range(4):
                    sl = wv % 2
                    kb.wait("sync", tok_wsl[sl])
                    t_l = kb.sig("sync", lambda e, dm=dm, wv=wv, sl=sl: e.dma_start(
                        out=bass.AP(wb.ap().tensor, sl * 2048,
                                    [[4096, 128], [1, 2048]]),
                        in_=bass.AP(wds_sp.tensor, (dm * 4 + wv) * 128 * 2048,
                                    [[2048, 128], [1, 2048]])),
                        dma=True)
                    kb.wait("tensor", t_l, t_pc,
                            tok_yb[bank] if wv == 0 else None)
                    for ki in range(16):
                        st = (wv == 0 and ki == 0)
                        sp = (wv == 3 and ki == 15)

                        def f(e, ki=ki, wv=wv, sl=sl, bank=bank, st=st, sp=sp):
                            return e.matmul(
                                pA[:, bank * 512: bank * 512 + 512],
                                wb[:, sl * 2048 + ki * 128: sl * 2048 + ki * 128 + 128],
                                xt[:, (wv * 16 + ki) * 512: (wv * 16 + ki) * 512 + 512],
                                start=st, stop=sp)
                        if ki == 15:
                            mml = kb.sig("tensor", f)
                        else:
                            kb.op("tensor", f)
                    tok_wsl[sl] = mml
                kb.wait("scalar", mml)
                t_ybf = kb.sig("scalar", lambda e, dm=dm, bank=bank: e.copy(
                    bigA[:, dm * 512: dm * 512 + 512],
                    pA[:, bank * 512: bank * 512 + 512]))
                tok_yb[bank] = t_ybf
            if ch + 1 < 4:
                ld_pc(ch + 1, mml)
            utoks = []
            tprev2 = None
            for dm in range(16):
                tb = fA if dm % 2 == 0 else fB
                kb.wait("vector", tok_yb[dm % 4], t_xr, tprev2)
                tu1 = kb.sig("vector", lambda e, dm=dm, tb=tb: e.tensor_tensor(
                    tb[:, 0:512], bigA[:, dm * 512: dm * 512 + 512],
                    bigA[:, 8192 + dm * 512: 8192 + dm * 512 + 512], ALU.add))
                kb.wait("vector", tu1)
                tu2 = kb.sig("vector", lambda e, dm=dm, tb=tb: e.tensor_copy(
                    bigA[:, dm * 512: dm * 512 + 512], tb[:, 0:512]))
                utoks.append(tu2)
                tprev2 = tu2
            if ch + 1 < 4:
                ld_xr(ch + 1, utoks[-1])
            tok_pe = emit_stats(
                lambda t: bigA[:, t * 512: t * 512 + 512], 512,
                pA[0:1, 0:512], pA[0:1, 512:1024], utoks=utoks)
            tok_m, tok_rb2 = emit_mr_rows_bc(
                tok_pe, pA[0:1, 0:512], pA[0:1, 512:1024], 512, 0, 2048,
                pA[:, 1024:1536], pA[:, 1536:2048])
            def ysl(sl):
                tb_ = fA if sl < 2 else fB
                return bass.AP(tb_.ap().tensor, (sl % 2) * 1024,
                               [[2048, 128], [1, 1024]]).bitcast(F32)

            tokdm = [None] * 4
            tsub_p = [None] * 4
            for t in range(NT):
                sl = t % 4
                kb.wait("vector", tok_m, tok_rb2, tsub_p[sl])
                t_s = kb.sig("vector", lambda e, t=t, sl=sl: e.tensor_tensor(
                    qfl(sl), bigA[:, t * 512: t * 512 + 512],
                    pA[:, 1024:1536], ALU.subtract))
                kb.wait("vector", t_s, tokdm[sl])
                t_n = kb.sig("vector", lambda e, sl=sl: e.tensor_tensor(
                    ysl(sl), qfl(sl), pA[:, 1536:2048], ALU.mult))
                tsub_p[sl] = t_n
                kb.wait("sync", t_n)
                tokdm[sl] = kb.sig("sync", lambda e, t=t, c0=c0, sl=sl: e.dma_start(
                    out=bass.AP(y_t.tensor, t * 128 * S + c0,
                                [[S, 128], [1, 512]]),
                    in_=ysl(sl)), dma=True)
            kb.barrier()

    _gen = dwn_gen()
    _gen_done = [False]

    def pump(n=1):
        for _ in range(n):
            if _gen_done[0]:
                return
            try:
                next(_gen)
            except StopIteration:
                _gen_done[0] = True
                return

    utoks1 = phase1(pump)
    while not _gen_done[0]:
        pump(10)
    kb.barrier()
    norm1 = emit_full_ln(utoks1, 0)
    kb.barrier()
    if stop != "p1":
        phase2a(tok_idn)
        kb.barrier()
        phase2b()
        if stop != "p2":
            utoks3 = phase3()
            norm3 = emit_full_ln(utoks3, 0)
            kb.barrier()
            if stop != "p3":
                phase4(stop)

    if stop is not None and stop != "full":
        kb.barrier()
        tokdmp = None
        for t in range(NT):
            for qd in range(4):
                kb.wait("vector", tokdmp)
                tk = kb.sig("vector", lambda e, t=t, qd=qd: e.tensor_copy(
                    qf[:], xt[:, t * S + qd * 512: t * S + qd * 512 + 512]))
                kb.wait("sync", tk)
                tokdmp = kb.sig("sync", lambda e, t=t, qd=qd: e.dma_start(
                    out=bass.AP(y_t.tensor, t * 128 * S + qd * 512,
                                [[S, 128], [1, 512]]),
                    in_=qf[:]), dma=True)

    for q in ("q_sp", "q_sc"):
        for r in range(KB.NQROT):
            nm = f"{q}{r}"
            if kb.cnt[nm] > 0:
                kb.wait("sync", (nm, kb.cnt[nm]))
    for r in range(KB.NQROT):
        nm = f"q_gp{r}"
        if kb.cnt[nm] > 0:
            kb.wait("gpsimd", (nm, kb.cnt[nm]))

    with nc.Block() as block:
        @block.sync
        def _(e):
            for f in kb.th["sync"]:
                f(e)

        @block.scalar
        def _(e):
            for f in kb.th["scalar"]:
                f(e)

        @block.vector
        def _(e):
            for f in kb.th["vector"]:
                f(e)

        @block.gpsimd
        def _(e):
            for f in kb.th["gpsimd"]:
                f(e)

        @block.tensor
        def _(e):
            for f in kb.th["tensor"]:
                f(e)

    es.__exit__(None, None, None)
    return nc


def _host_inputs(inputs):
    x = np.asarray(inputs["L0_attn_x"])
    def _tm(a):
        return np.ascontiguousarray(
            np.asarray(a).reshape(16, 128, 16, 128)
            .transpose(2, 1, 0, 3).reshape(2048, 2048))

    base = {
        "wq0t": _tm(np.asarray(inputs["L0_attn_Wq"]).T),
        "wk0t": _tm(np.asarray(inputs["L0_attn_Wk"]).T),
        "wv0t": _tm(np.asarray(inputs["L0_attn_Wv"]).T),
        "wo0t": _tm(np.asarray(inputs["L0_attn_Wo"]).T),
        "gup_pk": np.ascontiguousarray(np.asarray(inputs["L0_ffn_gate_up_packed"])),
        "gup_sc": np.ascontiguousarray(np.asarray(inputs["L0_ffn_gate_up_scales"]).T),
        "dwn_pk": np.ascontiguousarray(np.asarray(inputs["L0_ffn_down_packed"])),
        "dwn_sc": np.ascontiguousarray(np.asarray(inputs["L0_ffn_down_scales"]).T),
        "wq1": _tm(np.asarray(inputs["L1_attn_Wq"])),
        "wk1": _tm(np.asarray(inputs["L1_attn_Wk"])),
        "wv1": _tm(np.asarray(inputs["L1_attn_Wv"])),
        "wo1": _tm(np.asarray(inputs["L1_attn_Wo"])),
        "bo1": np.ascontiguousarray(np.asarray(inputs["L1_attn_bo"]).reshape(16, 128).T),
        "wg1t": np.ascontiguousarray(
            np.asarray(inputs["L1_ffn_W_g"]).reshape(64, 128, 16, 128)
            .transpose(0, 3, 2, 1).reshape(8192, 2048)),
        "wu1t": np.ascontiguousarray(
            np.asarray(inputs["L1_ffn_W_u"]).reshape(64, 128, 16, 128)
            .transpose(0, 3, 2, 1).reshape(8192, 2048)),
        "wd1t": np.ascontiguousarray(np.asarray(inputs["L1_ffn_W_d"]).T),
        "idn": np.eye(128, dtype=ml_dtypes.bfloat16),
        "idnf": np.eye(128, dtype=np.float32),
        "maskw": (np.arange(1024)[None, :] >= (np.arange(128)[:, None] + 512)
                  ).astype(ml_dtypes.bfloat16),
    }
    in_maps = []
    for c in range(8):
        m = dict(base)
        m["x_t"] = np.ascontiguousarray(np.asarray(x[c]).T)
        in_maps.append(m)
    return in_maps


def kernel(**inputs):
    in_maps = _host_inputs(inputs)
    nc = build_program()
    res = run_bass_kernel_spmd(nc, in_maps, core_ids=list(range(8)))
    out = np.stack([np.asarray(r["y_t"]).T for r in res.results], axis=0)
    return np.ascontiguousarray(out.astype(np.float32))
